# version 19 (final)
# speedup vs baseline: 1.0904x; 1.0077x over previous
import numpy as np

import concourse.bass as bass
import concourse.mybir as mybir
from concourse.bass import IndirectOffsetOnAxis
from concourse.bass_utils import run_bass_kernel_spmd

F32 = mybir.dt.float32
BF16 = mybir.dt.bfloat16
I32 = mybir.dt.int32
AF = mybir.ActivationFunctionType
ALU = mybir.AluOpType
AX = mybir.AxisListType

BIG = 65536.0


def MM(out, lhsT, rhs, start, stop):
    return lambda h, o=out, l=lhsT, r=rhs, s=start, t=stop: h.matmul(o, l, r, start=s, stop=t)


def TRANS(out, in_, ident):
    return lambda h, o=out, i=in_, d=ident: h.transpose(o, i, d)


def TT(out, a, b, op):
    return lambda h, o=out, x=a, y=b, p_=op: h.tensor_tensor(o, x, y, p_)


def TS(out, in_, s1, s2, op0, op1):
    return lambda h, o=out, i=in_, a=s1, b=s2, p0=op0, p1=op1: h.tensor_scalar(o, i, a, b, p0, p1)


def TS1(out, in_, s1, op):
    return lambda h, o=out, i=in_, a=s1, p_=op: h.tensor_single_scalar(o, i, a, p_)


def STT(out, in0, s, in1, op0, op1):
    return lambda h, o=out, i=in0, a=s, j=in1, p0=op0, p1=op1: h.scalar_tensor_tensor(o, i, a, j, p0, p1)


def CP(out, in_):
    return lambda h, o=out, i=in_: h.tensor_copy(o, i)


def ACP(out, in_):
    return lambda h, o=out, i=in_: h.copy(o, i)


def ACTV(out, in_, func, bias=0.0, scale=1.0, accum_out=None):
    return lambda h, o=out, i=in_, f=func, b=bias, s=scale, a=accum_out: (
        h.activation(o, i, f, bias=b, scale=s) if a is None
        else h.activation(o, i, f, bias=b, scale=s, accum_out=a))


def DMA(out, in_):
    return lambda h, o=out, i=in_: h.dma_start(out=o, in_=i)


def RSUM(out, in_):
    return lambda h, o=out, i=in_: h.reduce_sum(o, i, axis=AX.X)


def RMAX(out, in_):
    return lambda h, o=out, i=in_: h.reduce_max(o, i, axis=AX.X)


def RECIP(out, in_):
    return lambda h, o=out, i=in_: h.reciprocal(o, i)


def MEMSET(ap, v):
    return lambda h, o=ap, x=v: h.memset(o, x)


class Prog:
    ENG = {"PE": "tensor", "DVE": "vector", "ACT": "scalar", "POOL": "gpsimd", "SP": "sync"}
    DMA_QUEUES = ("SP", "ACT", "POOL")
    N_DMA_SEMS = 8

    def __init__(self, nc):
        self.nc = nc
        self.ops = {e: [] for e in self.ENG}
        self._sem_guards = []
        self.sems = {}
        self.cnt = {}
        self.watermark = {}
        self.buf = {}
        for e in self.ENG:
            self._mk_sem("c_" + e)
        for q in self.DMA_QUEUES:
            for i in range(self.N_DMA_SEMS):
                self._mk_sem(f"d_{q}{i}")
        self.dma_rr = {q: 0 for q in self.DMA_QUEUES}
        self.dma_prev_total = {}

    def _mk_sem(self, key):
        g = self.nc.semaphore(key)
        self.sems[key] = g.__enter__()
        self._sem_guards.append(g)
        self.cnt[key] = 0

    def _deps_for(self, reads, writes):
        deps = {}
        def add(sem_key, val):
            if val > deps.get(sem_key, 0):
                deps[sem_key] = val
        for b in reads:
            st = self.buf.get(b)
            if st and st["w"]:
                add(*st["w"])
            if st and (b.startswith("PS") or b.startswith("PB")):
                for k, v in st["r"].items():
                    add(k, v)
        for b in writes:
            st = self.buf.get(b)
            if st:
                if st["w"]:
                    add(*st["w"])
                for k, v in st["r"].items():
                    add(k, v)
        return deps

    def _emit_waits(self, eng, deps):
        for sem_key, val in sorted(deps.items()):
            if val <= self.watermark.get((eng, sem_key), 0):
                continue
            self.watermark[(eng, sem_key)] = val
            self.ops[eng].append(lambda h, s=self.sems[sem_key], v=val: h.wait_ge(s, v))

    def _record(self, reads, writes, sem_key, val):
        for b in writes:
            self.buf[b] = {"w": (sem_key, val), "r": {}}
        for b in reads:
            st = self.buf.setdefault(b, {"w": None, "r": {}})
            if val > st["r"].get(sem_key, 0):
                st["r"][sem_key] = val

    def op(self, eng, fn, reads=(), writes=()):
        import traceback
        _tb = traceback.extract_stack()[-2]
        fns = fn if isinstance(fn, (list, tuple)) else [fn]
        sem_key = "c_" + eng
        self._emit_waits(eng, self._deps_for(reads, writes))
        val = self.cnt[sem_key] + 1
        sem = self.sems[sem_key]
        for i, f in enumerate(fns):
            def run1(h, f=f, s=sem, last=(i == len(fns) - 1), tb=_tb):
                try:
                    inst = f(h)
                    inst.annotate(f"{tb.filename.split('/')[-1]}:{tb.lineno}")
                    if last:
                        inst.then_inc(s, 1)
                except Exception as e:
                    raise RuntimeError(f"op recorded at {tb.filename}:{tb.lineno}") from e
            self.ops[eng].append(run1)
        self.cnt[sem_key] = val
        self._record(reads, writes, sem_key, val)
        return val

    def dma(self, queue, fn, reads=(), writes=()):
        fns = fn if isinstance(fn, (list, tuple)) else [fn]
        i = self.dma_rr[queue]
        self.dma_rr[queue] = (i + 1) % self.N_DMA_SEMS
        sem_key = f"d_{queue}{i}"
        sem = self.sems[sem_key]
        self._emit_waits(queue, self._deps_for(reads, writes))
        prev = self.dma_prev_total.get(sem_key)
        if prev is not None and prev > self.watermark.get((queue, sem_key), 0):
            self.watermark[(queue, sem_key)] = prev
            self.ops[queue].append(lambda h, s=sem, v=prev: h.wait_ge(s, v))
        total = self.cnt[sem_key] + 16 * len(fns)
        for f in fns:
            self.ops[queue].append(lambda h, f=f, s=sem: f(h).then_inc(s, 16))
        self.cnt[sem_key] = total
        self.dma_prev_total[sem_key] = total
        self._record(reads, writes, sem_key, total)
        return total

    def wait_all_done(self):
        deps = {k: v for k, v in self.cnt.items() if v > 0}
        self._emit_waits("SP", deps)

    def finalize(self):
        nc = self.nc
        with nc.Block() as block:
            for eng, attr in self.ENG.items():
                ops = self.ops[eng]
                if not ops:
                    continue
                def body(h, ops=ops):
                    for f in ops:
                        f(h)
                getattr(block, attr)(body)
        for g in self._sem_guards:
            g.__exit__(None, None, None)
        self._sem_guards = []


def pad_for(T):
    return min(T, max(256, ((T * 5 // 8) + 255) // 256 * 256))


def prep_host(ins, T):
    f32 = np.float32
    def bc(v, p=128):
        return np.ascontiguousarray(np.broadcast_to(np.asarray(v, f32), (p, len(v))))
    d = {}
    d["wqT"] = np.ascontiguousarray(ins["L0_attn_Wq"].T)
    d["wkT"] = np.ascontiguousarray(ins["L0_attn_Wk"].T)
    d["wvT"] = np.ascontiguousarray(ins["L0_attn_Wv"].T)
    d["wbT"] = np.ascontiguousarray(ins["L0_attn_Wb"].T)
    d["bbcol"] = np.ascontiguousarray(ins["L0_attn_bb"].reshape(16, 1))
    d["woT"] = np.ascontiguousarray(ins["L0_attn_Wo"].T)
    d["boB"] = bc(ins["L0_attn_bo"])
    d["scaleB"] = bc(ins["L0_attn_scale"])
    d["C0"] = np.ascontiguousarray(ins["L0_ffn_controller"].reshape(1024, 64))
    d["F1"] = np.ascontiguousarray(ins["L0_ffn_f1"].reshape(1024, 4096))
    d["F2"] = np.ascontiguousarray(ins["L0_ffn_f2"].reshape(4096, 1024))
    d["fbcols"] = np.ascontiguousarray(ins["L0_ffn_bias"].reshape(4096).reshape(32, 128).T)
    d["w_q1"] = np.ascontiguousarray(ins["L1_attn_w_q"])
    d["w_k1"] = np.ascontiguousarray(ins["L1_attn_w_k"])
    d["w_v1"] = np.ascontiguousarray(ins["L1_attn_w_v"])
    d["w_o1"] = np.ascontiguousarray(ins["L1_attn_w_o"])
    d["hdm"] = np.ascontiguousarray(ins["L1_attn_head_dim_mask"].reshape(128, 1))
    d["wqmT"] = np.ascontiguousarray(ins["L1_ffn_Wq"].T)
    d["wdkvT"] = np.ascontiguousarray(ins["L1_ffn_Wdkv"].T)
    d["wukT"] = np.ascontiguousarray(ins["L1_ffn_Wuk"].T)
    d["wuvT"] = np.ascontiguousarray(ins["L1_ffn_Wuv"].T)
    d["womT"] = np.ascontiguousarray(ins["L1_ffn_Wo"].T)
    d["n1wB"] = bc(ins["L1_ffn_norm1_w"])
    d["n2wB"] = bc(ins["L1_ffn_norm2_w"])
    d["lngB"] = bc(ins["L1_ffn_ln_g"])
    d["lnbB"] = bc(ins["L1_ffn_ln_b"])
    d["gwT"] = np.ascontiguousarray(ins["L1_ffn_gate_w"].T)
    d["gbB"] = bc(ins["L1_ffn_gate_b"])
    d["W1T"] = np.ascontiguousarray(ins["L1_ffn_W1"].transpose(0, 2, 1))
    d["b1cols"] = np.ascontiguousarray(ins["L1_ffn_b1"].reshape(4, 32, 128).transpose(0, 2, 1))
    d["W2T"] = np.ascontiguousarray(ins["L1_ffn_W2"].transpose(0, 2, 1))
    d["b2"] = np.ascontiguousarray(ins["L1_ffn_b2"])
    d["IDENT"] = np.eye(128, dtype=f32)
    jj, ii = np.mgrid[0:128, 0:128]
    d["TRI"] = (jj <= ii).astype(f32)
    d["REPLBIG"] = np.ascontiguousarray(
        (BIG * (np.arange(4096)[None, :] // 64 == np.arange(64)[:, None])).astype(f32))
    d["TIE16"] = np.ascontiguousarray(
        np.broadcast_to(np.linspace(0.0, 1e-6, 16, dtype=f32), (64, 16)))
    d["e127"] = np.zeros((128, 1), f32); d["e127"][127, 0] = 1.0
    big = np.zeros((4, 512), f32)
    for e in range(4):
        big[e, 128 * e: 128 * (e + 1)] = BIG
    d["BIGROWS"] = big
    PAD = pad_for(T)
    d["TIDS"] = np.ascontiguousarray(np.arange(T, dtype=np.int32).reshape(T // 128, 128).T)
    d["BIGIDX"] = np.full((4 * (PAD + 1), 1), 1 << 30, np.int32)
    d["B2ROWS"] = np.ascontiguousarray(ins["L1_ffn_b2"].reshape(1, 4096))
    d["EOFF"] = (np.arange(4, dtype=f32) * (PAD + 1)).reshape(4, 1)
    inv_freq = (1.0 / 10000.0 ** (np.arange(32, dtype=f32) / 32.0)).astype(f32)
    tt = np.arange(T, dtype=f32)
    ang = tt[:, None] * inv_freq[None, :]
    cosv, sinv = np.cos(ang).astype(f32), np.sin(ang).astype(f32)
    nt = T // 128
    d["COSt"] = np.ascontiguousarray(cosv.reshape(nt, 128, 32).transpose(1, 0, 2).reshape(128, nt * 32))
    d["SINt"] = np.ascontiguousarray(sinv.reshape(nt, 128, 32).transpose(1, 0, 2).reshape(128, nt * 32))
    return d


HOST_SPECS = {
    "wqT": ((1024, 1024), F32), "wkT": ((1024, 1024), F32), "wvT": ((1024, 1024), F32),
    "wbT": ((1024, 16), F32), "bbcol": ((16, 1), F32), "woT": ((1024, 1024), F32),
    "boB": ((128, 1024), F32), "scaleB": ((128, 1024), F32),
    "C0": ((1024, 64), F32), "F1": ((1024, 4096), F32), "F2": ((4096, 1024), F32),
    "fbcols": ((128, 32), F32),
    "w_q1": ((1024, 128), F32), "w_k1": ((1024, 128), F32), "w_v1": ((1024, 128), F32),
    "w_o1": ((128, 1024), F32), "hdm": ((128, 1), I32),
    "wqmT": ((1024, 1024), F32), "wdkvT": ((1024, 512), F32),
    "wukT": ((512, 1024), F32), "wuvT": ((512, 1024), F32), "womT": ((1024, 1024), F32),
    "n1wB": ((128, 1024), F32), "n2wB": ((128, 1024), F32),
    "lngB": ((128, 512), F32), "lnbB": ((128, 512), F32),
    "gwT": ((1024, 4), F32), "gbB": ((128, 4), F32),
    "W1T": ((4, 1024, 4096), F32), "b1cols": ((4, 128, 32), F32),
    "W2T": ((4, 4096, 1024), F32), "b2": ((4, 1024), F32),
    "IDENT": ((128, 128), F32), "TRI": ((128, 128), F32),
    "REPLBIG": ((64, 4096), F32), "TIE16": ((64, 16), F32), "e127": ((128, 1), F32),
    "BIGROWS": ((4, 512), F32), "TIDS": None, "BIGIDX": None, "B2ROWS": ((1, 4096), F32),
    "EOFF": ((4, 1), F32),
    "COSt": None, "SINt": None,
}


class Ctx:
    def __init__(self, nc, T, SC=None):
        self.nc = nc
        self.T = T
        self.SC = SC or min(1024, T)
        self.p = Prog(nc)
        self.hb = {}
        self.sb = {}
        self.ps = {}

    def dram(self, name, shape, dtype, kind):
        ap = self.nc.dram_tensor(name, list(shape), dtype, kind=kind).ap()
        self.hb[name] = ap
        return ap

    def sbuf(self, name, shape, dtype):
        ap = self.nc.alloc_sbuf_tensor(name, list(shape), dtype).ap()
        self.sb[name] = ap
        return ap

    def psum(self, name, shape, dtype=F32):
        ap = self.nc.alloc_psum_tensor(name, list(shape), dtype).ap()
        self.ps[name] = ap
        return ap


def declare_io(cx, full_l0=True):
    T = cx.T
    cx.dram("x", (T, 1024), F32, "ExternalInput")
    for name, spec in HOST_SPECS.items():
        if name in ("COSt", "SINt"):
            cx.dram(name, (128, (T // 128) * 32), F32, "ExternalInput")
        elif name == "TIDS":
            cx.dram(name, (128, T // 128), I32, "ExternalInput")
        elif name == "BIGIDX":
            cx.dram(name, (4 * (pad_for(T) + 1), 1), I32, "ExternalInput")
        else:
            cx.dram(name, spec[0], spec[1], "ExternalInput")


def declare_scratch(cx, names, kind="Internal"):
    for n in names:
        cx.dram(n, (cx.T, 1024), F32, kind)


def alloc_globals(cx):
    T = cx.T
    nt = T // 128
    for i in range(4):
        cx.psum(f"PS{i}", (128, 512))
    cx.psum("PB0", (128, 1024))
    cx.psum("PB1", (128, 1024))
    cx.sbuf("ident", (128, 128), F32)
    cx.sbuf("tribf", (128, 128), BF16)
    nch = (cx.SC if hasattr(cx, "SC") else min(1024, T)) // 128
    cx.sbuf("cost", (128, nch * 32), F32)
    cx.sbuf("sint", (128, nch * 32), F32)
    cx.sbuf("onesbf", (128, 1), BF16)
    cx.sbuf("onesf", (128, 1), F32)
    cx.sbuf("onesrow", (1, 128), F32)
    cx.sbuf("onesrbf", (1, 128), BF16)
    cx.sbuf("zrow16", (16, 128), F32)
    cx.sbuf("epscol", (128, 1), F32)
    cx.sbuf("eoffc", (128, 1), F32)
    cx.sbuf("c_m4", (128, 1), F32)
    cx.sbuf("e127s", (128, 1), F32)
    for i in range(4):
        cx.sbuf(f"col{i}", (128, 1), F32)


def emit_load_consts(cx):
    p, hb, sb = cx.p, cx.hb, cx.sb
    p.dma("SP", [
        lambda h: h.dma_start(out=sb["ident"], in_=hb["IDENT"][:]),
        lambda h: h.dma_start(out=sb["e127s"], in_=hb["e127"][:]),
    ], writes=["ident", "e127s"])
    p.dma("POOL", [lambda h: h.dma_start(out=sb["tribf"], in_=hb["TRI"][:])], writes=["tribf"])
    p.op("DVE", lambda h: h.tensor_copy(sb["identbf"], sb["ident"]), reads=["ident"], writes=["identbf"])
    p.op("DVE", lambda h: h.memset(sb["onesbf"], 1.0), writes=["onesbf"])
    p.op("DVE", lambda h: h.memset(sb["onesf"], 1.0), writes=["onesf"])
    p.op("DVE", lambda h: h.memset(sb["onesrow"], 1.0), writes=["onesrow"])
    p.op("DVE", lambda h: h.memset(sb["onesrbf"], 1.0), writes=["onesrbf"])
    p.op("DVE", lambda h: h.memset(sb["zrow16"], 0.0), writes=["zrow16"])
    p.op("DVE", lambda h: h.memset(sb["c_m4"], -4.0), writes=["c_m4"])


def alloc_l0(cx):
    T = cx.T
    SC = cx.SC
    NCH = SC // 128
    cx.sbuf("XF32", (128, 8 * SC), F32)
    cx.sbuf("WHM", (128, 4 * 8192), BF16)
    cx.sbuf("SCR", (128, 8 * 1024), F32)
    cx.sbuf("SL2", (128, 33280), BF16)
    cx.sbuf("cumT", (16, 128), F32)
    cx.sbuf("cumtm", (128, 16), F32)
    cx.sbuf("rctm", (128, 16), F32)
    cx.sbuf("kcs", (128, 16), F32)
    cx.sbuf("cumcS", (128, 8), F32)
    cx.sbuf("cumcS2", (128, 8), F32)
    cx.sbuf("cumcrow", (1, 16), F32)
    cx.sbuf("stats", (128, 2 * NCH), F32)
    cx.sbuf("lgt", (64, 128), F32)
    cx.sbuf("gmax8", (64, 8), F32)
    cx.sbuf("tiec", (64, 16), F32)
    cx.sbuf("wbTs", (128, 8 * 16), F32)
    cx.sbuf("bbcols", (16, 1), F32)
    cx.sbuf("fbcolss", (128, 32), F32)
    cx.sbuf("identbf", (128, 128), BF16)
    cx.sbuf("Sall", (128, 1024), F32)


class L0View:
    def __init__(self, cx, SC):
        sb = cx.sb
        sl2 = sb["SL2"]
        f32v = sl2.bitcast(F32)
        b = 8 * SC
        self.xnfmbf = sl2[:, 0:8 * SC]
        self.qsbf = sl2[:, b:b + 1024]
        self.ksbf = sl2[:, b + 1024:b + 2048]
        self.kcbf = sl2[:, b + 2048:b + 3072]
        self.vbf = sl2[:, b + 3072:b + 4096]
        self.qsfm = sl2[:, b + 4096:b + 5120]
        self.ksfm = sl2[:, b + 5120:b + 6144]
        self.atp = sl2[:, b + 6144:b + 7168]
        self.attn_t = sl2[:, b + 7168:b + 8192]
        self.attnfm = sl2[:, b + 8192:b + 9216]
        self.f1buf = sl2[:, b + 9216:b + 11264]
        self.f2buf = sl2[:, b + 11264:b + 13312]
        self.replbf = sl2[64:128, b + 13312:b + 17408]
        fb = (b + 17408) // 2
        self.Sbf = sl2[:, b + 22528:b + 23552]
        self.m1bf = sl2[64:128, b + 23552:b + 24576]
        self.betaT = f32v[0:16, (b + 13312) // 2: (b + 13312) // 2 + SC]
        self.boBs = f32v[:, fb:fb + 1024]
        self.scaleBs = f32v[:, fb + 1024:fb + 2048]
        self.C0s = f32v[:, fb + 2048:fb + 2560]
        self.kmap = {}
        names = ["xnfmbf", "qsbf", "ksbf", "kcbf", "vbf", "qsfm", "ksfm", "atp",
                 "attn_t", "attnfm", "f1buf", "f2buf", "replbf"]
        off = {"xnfmbf": (0, 8 * SC), "qsbf": (b, b + 1024), "ksbf": (b + 1024, b + 2048),
               "kcbf": (b + 2048, b + 3072), "vbf": (b + 3072, b + 4096),
               "qsfm": (b + 4096, b + 5120), "ksfm": (b + 5120, b + 6144),
               "atp": (b + 6144, b + 7168), "attn_t": (b + 7168, b + 8192),
               "attnfm": (b + 8192, b + 9216), "f1buf": (b + 9216, b + 11264),
               "f2buf": (b + 11264, b + 13312), "replbf": (b + 13312, b + 17408)}
        off["Sbf"] = (b + 22528, b + 23552)
        off["m1bf"] = (b + 23552, b + 24576)
        off["betaT"] = (b + 13312, b + 13312 + 2 * SC)
        off["boBs"] = (b + 17408, b + 19456)
        off["scaleBs"] = (b + 19456, b + 21504)
        off["C0s"] = (b + 21504, b + 22528)
        self.off = off

    def keys(self, name, lo=None, hi=None):
        o0, o1 = self.off[name]
        if lo is not None:
            o1 = o0 + hi
            o0 = o0 + lo
        return [f"SL2.{bb}" for bb in range(o0 // 1024, (o1 + 1023) // 1024)]


def emit_l0(cx, stop=None):
    p, hb, sb, ps = cx.p, cx.hb, cx.sb, cx.ps
    T = cx.T
    SC = cx.SC
    NSC, NCH = T // SC, SC // 128
    EPS = 1e-6
    v = L0View(cx, SC)

    def K(slab, lo, hi):
        return [f"{slab}.{b}" for b in range(lo // 1024, (hi + 1023) // 1024)]

    p.dma("SP", [
        DMA(v.C0s.rearrange("p (c e) -> p c e", c=8), hb["C0"].rearrange("(c p) e -> p c e", p=128)),
        DMA(sb["wbTs"].rearrange("p (c e) -> p c e", c=8), hb["wbT"].rearrange("(c p) e -> p c e", p=128)),
        DMA(sb["bbcols"], hb["bbcol"][:]),
        DMA(sb["fbcolss"], hb["fbcols"][:]),
        DMA(v.boBs, hb["boB"][:]),
        DMA(v.scaleBs, hb["scaleB"][:]),
        DMA(sb["tiec"], hb["TIE16"][:]),
    ], writes=["wbTs", "bbcols", "fbcolss", "tiec"] + v.keys("C0s") + v.keys("boBs") + v.keys("scaleBs"))
    p.dma("POOL", [DMA(v.replbf, hb["REPLBIG"][:])], writes=v.keys("replbf"))
    p.op("DVE", MEMSET(sb["epscol"], 1e-6), writes=["epscol"])
    p.op("DVE", MEMSET(sb["Sall"], 0.0), writes=["Sall"])
    p.op("DVE", MEMSET(ps["PB1"], 0.0), writes=["PB1"])
    p.op("DVE", MEMSET(v.Sbf, 0.0), writes=v.keys("Sbf"))

    xnall = v.keys("xnfmbf")
    if stop == "consts":
        return

    for sc in range(NSC):
        t0 = sc * SC
        for wi, wname in enumerate(["wqT", "wkT", "wvT", "woT"]):
            p.dma("POOL", [DMA(sb["WHM"][:, 8192 * wi: 8192 * (wi + 1)].rearrange("p (c d) -> p c d", c=8),
                               hb[wname].rearrange("(c p) d -> p c d", p=128))],
                  writes=K("WHM", 8192 * wi, 8192 * (wi + 1)))

        p.dma("SP", [DMA(sb["cost"], hb["COSt"][:, 32 * NCH * sc: 32 * NCH * (sc + 1)]),
                     DMA(sb["sint"], hb["SINt"][:, 32 * NCH * sc: 32 * NCH * (sc + 1)])],
              writes=["cost", "sint"])
        xtP = [sb["SCR"][:, 0:1024], sb["SCR"][:, 4096:5120]]
        xfP = [sb["SCR"][:, 3072:4096], sb["SCR"][:, 5120:6144]]
        kxP = [K("SCR", 0, 1024), K("SCR", 4096, 5120)]
        kfP = [K("SCR", 3072, 4096), K("SCR", 5120, 6144)]
        xn = sb["SCR"][:, 1024:2048]
        sq = sb["SCR"][:, 2048:3072]
        kn, ksq = K("SCR", 1024, 2048), K("SCR", 2048, 3072)
        colP = [(sb["col0"], sb["col1"]), (sb["col2"], sb["col3"])]
        kcolP = [("col0", "col1"), ("col2", "col3")]

        def ln_s1(ti):
            pr = ti % 2
            xt, kx = xtP[pr], kxP[pr]
            c0, c1 = colP[pr]
            k0, k1 = kcolP[pr]
            p.dma("SP", [DMA(xt, hb["x"][t0 + 128 * ti: t0 + 128 * (ti + 1), :])], writes=kx)
            mu = sb["stats"][:, 2 * ti: 2 * ti + 1]
            rs = sb["stats"][:, 2 * ti + 1: 2 * ti + 2]
            gti = t0 // 128 + ti
            p.op("DVE", RSUM(c0, xt), reads=kx, writes=[k0])
            p.op("ACT", ACTV(sq, xt, AF.Square, accum_out=c1), reads=kx, writes=ksq + [k1])
            p.op("DVE", TS1(mu, c0, 1.0 / 1024.0, ALU.mult), reads=[k0], writes=[f"st.{gti}"])
            p.op("DVE", TT(c0, mu, mu, ALU.mult), reads=[f"st.{gti}"], writes=[k0])
            p.op("DVE", TS(c1, c1, 1.0 / 1024.0, 1e-5, ALU.mult, ALU.add), reads=[k1], writes=[k1])
            p.op("DVE", TT(c1, c1, c0, ALU.subtract), reads=[k0, k1], writes=[k1])
            p.op("ACT", ACTV(c1, c1, AF.Ln), reads=[k1], writes=[k1])
            p.op("ACT", ACTV(rs, c1, AF.Exp, scale=-0.5), reads=[k1], writes=[f"st.{gti}"])

        def ln_s2(ti):
            pr = ti % 2
            xt, kx = xtP[pr], kxP[pr]
            xf, kf = xfP[pr], kfP[pr]
            mu = sb["stats"][:, 2 * ti: 2 * ti + 1]
            rs = sb["stats"][:, 2 * ti + 1: 2 * ti + 2]
            gti = t0 // 128 + ti
            p.op("DVE", TS(xn, xt, mu, rs, ALU.subtract, ALU.mult),
                 reads=kx + [f"st.{gti}"], writes=kn)
            if stop == "stats":
                return
            for ccb in (0, 4):
                grp = [TRANS(ps["PB0"][:, 128 * (cc - ccb): 128 * (cc - ccb + 1)],
                             xn[:, 128 * cc: 128 * (cc + 1)], sb["ident"])
                       for cc in range(ccb, ccb + 4)]
                p.op("PE", grp, reads=kn + ["ident"], writes=["PB0.a"])
                for cc in range(ccb, ccb + 4):
                    srcp = ps["PB0"][:, 128 * (cc - ccb): 128 * (cc - ccb + 1)]
                    p.op("DVE", CP(xf[:, 128 * cc: 128 * (cc + 1)], srcp), reads=["PB0.a"], writes=kf)
                    p.op("ACT", ACP(v.xnfmbf[:, SC * cc + 128 * ti: SC * cc + 128 * (ti + 1)], srcp),
                         reads=["PB0.a"], writes=v.keys("xnfmbf", SC * cc, SC * cc + SC))
            if stop == "xpose":
                return
            grp = [MM(ps["PS0"][0:16, 0:128], sb["wbTs"][:, 16 * mc: 16 * (mc + 1)],
                      xf[:, 128 * mc: 128 * (mc + 1)], mc == 0, mc == 7) for mc in range(8)]
            p.op("PE", grp, reads=["wbTs"] + kf, writes=["PS0"])
            dstb = v.betaT[:, 128 * ti: 128 * (ti + 1)]
            p.op("ACT", ACTV(dstb, ps["PS0"][0:16, 0:128], AF.Sigmoid, bias=sb["bbcols"]),
                 reads=["PS0", "bbcols"], writes=v.keys("betaT"))
            p.op("DVE", TS(dstb, dstb, 0.8, 0.999, ALU.max, ALU.min), reads=v.keys("betaT"), writes=v.keys("betaT"))
            if stop == "betaln":
                return
            grp = [MM(ps["PS1"][0:64, 0:128], v.C0s[:, 64 * mc: 64 * (mc + 1)],
                      xf[:, 128 * mc: 128 * (mc + 1)], mc == 0, mc == 7) for mc in range(8)]
            p.op("PE", grp, reads=v.keys("C0s") + kf, writes=["PS1"])
            p.op("DVE", TT(sb["lgt"].rearrange("p (g c) -> p g c", c=16),
                           ps["PS1"][0:64, 0:128].rearrange("p (g c) -> p g c", c=16),
                           sb["tiec"].unsqueeze(1).broadcast_to([64, 8, 16]), ALU.add),
                 reads=["PS1", "tiec"], writes=["lgt"])
            p.op("DVE", RMAX(sb["gmax8"].rearrange("p (g o) -> p g o", o=1),
                             sb["lgt"].rearrange("p (g c) -> p g c", c=16)),
                 reads=["lgt"], writes=["gmax8"])
            m1d = v.m1bf[:, 128 * ti: 128 * (ti + 1)]
            p.op("DVE", TT(m1d.rearrange("p (g c) -> p g c", c=16),
                           sb["lgt"].rearrange("p (g c) -> p g c", c=16),
                           sb["gmax8"].unsqueeze(2).broadcast_to([64, 8, 16]), ALU.is_ge),
                 reads=["lgt", "gmax8"], writes=[*v.keys("m1bf")])
            p.op("DVE", TS1(m1d, m1d, -1.0, ALU.add), reads=[*v.keys("m1bf")], writes=[*v.keys("m1bf")])

        ln_s1(0)
        for ti in range(NCH):
            if ti + 1 < NCH:
                ln_s1(ti + 1)
            ln_s2(ti)

        if stop == "ln":
            return
        c0, c1 = sb["col0"], sb["col1"]
        qt = sb["SCR"][:, 1024:2048]
        kt = sb["SCR"][:, 2048:3072]
        tA = sb["SCR"][:, 3072:4096]
        tB = sb["SCR"][:, 4096:5120]
        ht_ = sb["SCR"][:, 5120:6144]
        kq, kk = K("SCR", 1024, 2048), K("SCR", 2048, 3072)
        kA, kB, kh = K("SCR", 3072, 4096), K("SCR", 4096, 5120), K("SCR", 5120, 6144)
        b0 = 8 * SC
        QSFM = [v.qsfm, sb["SL2"][:, b0 + 9216:b0 + 10240]]
        KSFM = [v.ksfm, sb["SL2"][:, b0 + 10240:b0 + 11264]]
        KCBF = [v.kcbf, sb["SL2"][:, b0 + 11264:b0 + 12288]]
        VBF = [v.vbf, sb["SL2"][:, b0 + 12288:b0 + 13312]]
        kQSFM = [v.keys("qsfm"), v.keys("f1buf", 0, 1024)]
        kKSFM = [v.keys("ksfm"), v.keys("f1buf", 1024, 2048)]
        kKCBF = [v.keys("kcbf"), v.keys("f2buf", 0, 1024)]
        kVBF = [v.keys("vbf"), v.keys("f2buf", 1024, 2048)]
        CUMCS = [sb["cumcS"], sb["cumcS2"]]

        def stage_a(c):
            bi = c % 2
            cg = sc * NCH + c
            p.op("DVE", lambda h, o=sb["cumT"], d0=v.betaT[:, 128 * c: 128 * (c + 1)],
                 d1=sb["zrow16"]: h.tensor_tensor_scan(o, d0, d1, 1.0, ALU.mult, ALU.add),
                 reads=v.keys("betaT") + ["zrow16"], writes=["cumT"])
            p.op("PE", TRANS(ps["PB0"][:, 512:528], sb["cumT"], sb["ident"][0:16, 0:16]),
                 reads=["cumT", "ident"], writes=["PB0.b"])
            p.op("DVE", CP(sb["cumtm"], ps["PB0"][:, 512:528]), reads=["PB0.b"], writes=["cumtm"])
            p.op("PE", MM(ps["PB0"][0:1, 528:544], sb["e127s"], sb["cumtm"], True, True),
                 reads=["e127s", "cumtm"], writes=["PB0.b"])
            p.op("DVE", CP(sb["cumcrow"], ps["PB0"][0:1, 528:544]), reads=["PB0.b"], writes=["cumcrow"])
            p.op("PE", [MM(ps["PB0"][0:64, 544:552], sb["onesrow"][:, 0:64],
                           sb["cumcrow"][:, 0:16:2], True, True),
                        MM(ps["PB0"][64:128, 544:552], sb["onesrow"][:, 0:64],
                           sb["cumcrow"][:, 1:16:2], True, True)],
                 reads=["onesrow", "cumcrow"], writes=["PB0.b"])
            p.op("DVE", CP(CUMCS[bi], ps["PB0"][:, 544:552]), reads=["PB0.b"], writes=[f"cumcS{bi}"])
            p.op("DVE", RECIP(sb["rctm"], sb["cumtm"]), reads=["cumtm"], writes=["rctm"])
            p.op("PE", MM(ps["PB0"][:, 552:568], sb["onesrow"], sb["cumcrow"], True, True),
                 reads=["onesrow", "cumcrow"], writes=["PB0.b"])
            p.op("DVE", TT(sb["kcs"], ps["PB0"][:, 552:568], sb["rctm"], ALU.mult),
                 reads=["PB0.b", "rctm"], writes=["kcs"])
            for pi, (wi, outp, keys) in enumerate([(0, qt, kq), (1, kt, kk)]):
                for dh in range(2):
                    half = "ab"[dh]
                    grp = [MM(ps["PB0"][:, 512 * dh: 512 * (dh + 1)],
                              v.xnfmbf[:, SC * mc + 128 * c: SC * mc + 128 * (c + 1)],
                              sb["WHM"][:, 8192 * wi + 1024 * mc + 512 * dh:
                                        8192 * wi + 1024 * mc + 512 * (dh + 1)],
                              mc == 0, mc == 7) for mc in range(8)]
                    p.op("PE", grp, reads=xnall + K("WHM", 8192 * wi, 8192 * (wi + 1)),
                         writes=[f"PB0.{half}"])
                for dh in range(2):
                    half = "ab"[dh]
                    dst = outp[:, 512 * dh: 512 * (dh + 1)]
                    if pi == 0:
                        p.op("DVE", CP(dst, ps["PB0"][:, 512 * dh: 512 * (dh + 1)]),
                             reads=[f"PB0.{half}"], writes=keys)
                    else:
                        p.op("ACT", ACP(dst, ps["PB0"][:, 512 * dh: 512 * (dh + 1)]),
                             reads=[f"PB0.{half}"], writes=keys)
            for dh in range(2):
                half = "ab"[dh]
                grp = [MM(ps["PB0"][:, 512 * dh: 512 * (dh + 1)],
                          v.xnfmbf[:, SC * mc + 128 * c: SC * mc + 128 * (c + 1)],
                          sb["WHM"][:, 8192 * 2 + 1024 * mc + 512 * dh:
                                    8192 * 2 + 1024 * mc + 512 * (dh + 1)],
                          mc == 0, mc == 7) for mc in range(8)]
                p.op("PE", grp, reads=xnall + K("WHM", 8192 * 2, 8192 * 3), writes=[f"PB0.{half}"])
                p.op("DVE", CP(VBF[bi][:, 512 * dh: 512 * (dh + 1)], ps["PB0"][:, 512 * dh: 512 * (dh + 1)]),
                     reads=[f"PB0.{half}"], writes=kVBF[bi])
            cosv = sb["cost"][:, 32 * c: 32 * (c + 1)].unsqueeze(1).broadcast_to([128, 16, 32])
            sinv = sb["sint"][:, 32 * c: 32 * (c + 1)].unsqueeze(1).broadcast_to([128, 16, 32])
            for src_, keys in ((qt, kq), (kt, kk)):
                s3 = src_.rearrange("p (h d) -> p h d", d=64)
                a3 = tA.rearrange("p (h d) -> p h d", d=64)
                b3 = tB.rearrange("p (h d) -> p h d", d=64)
                p.op("DVE", TT(a3[:, :, 0:32], s3[:, :, 0:32], cosv, ALU.mult),
                     reads=keys + ["cost"], writes=kA)
                p.op("DVE", TT(a3[:, :, 32:64], s3[:, :, 32:64], sinv, ALU.mult),
                     reads=keys + ["sint"], writes=kA)
                p.op("DVE", TT(b3[:, :, 0:32], s3[:, :, 32:64], cosv, ALU.mult),
                     reads=keys + ["cost"], writes=kB)
                p.op("DVE", TT(b3[:, :, 32:64], s3[:, :, 0:32], sinv, ALU.mult),
                     reads=keys + ["sint"], writes=kB)
                p.op("DVE", TT(s3[:, :, 0:32], a3[:, :, 0:32], a3[:, :, 32:64], ALU.subtract),
                     reads=kA, writes=keys)
                p.op("DVE", TT(s3[:, :, 32:64], b3[:, :, 0:32], b3[:, :, 32:64], ALU.add),
                     reads=kB, writes=keys)
                p.op("ACT", ACTV(tB, src_, AF.Exp), reads=keys, writes=kB)
                p.op("DVE", TS1(tB, tB, 1.0, ALU.min), reads=kB, writes=kB)
                p.op("DVE", STT(src_, src_, 0.0, tB, ALU.max, ALU.add),
                     reads=keys + kB, writes=keys)
            cum3 = sb["cumtm"].unsqueeze(2).broadcast_to([128, 16, 64])
            rc3 = sb["rctm"].unsqueeze(2).broadcast_to([128, 16, 64])
            kcs3 = sb["kcs"].unsqueeze(2).broadcast_to([128, 16, 64])
            p.op("DVE", TT(v.qsbf.rearrange("p (h d) -> p h d", d=64),
                           qt.rearrange("p (h d) -> p h d", d=64), cum3, ALU.mult),
                 reads=kq + ["cumtm"], writes=v.keys("qsbf"))
            p.op("DVE", TT(v.ksbf.rearrange("p (h d) -> p h d", d=64),
                           kt.rearrange("p (h d) -> p h d", d=64), rc3, ALU.mult),
                 reads=kk + ["rctm"], writes=v.keys("ksbf"))
            p.op("DVE", TT(KCBF[bi].rearrange("p (h d) -> p h d", d=64),
                           kt.rearrange("p (h d) -> p h d", d=64), kcs3, ALU.mult),
                 reads=kk + ["kcs"], writes=kKCBF[bi])
            for src_t, kin, dsts, kouts in ((v.qsbf, v.keys("qsbf"), QSFM[bi], kQSFM[bi]),
                                            (v.ksbf, v.keys("ksbf"), KSFM[bi], kKSFM[bi])):
                for ccb in (0, 4):
                    half = "ab"[ccb // 4]
                    grp = [TRANS(ps["PB0"].bitcast(BF16)[:, 1024 * (ccb // 4) + 128 * (cc - ccb):
                                                         1024 * (ccb // 4) + 128 * (cc - ccb + 1)],
                                 src_t[:, 128 * cc: 128 * (cc + 1)], sb["identbf"])
                           for cc in range(ccb, ccb + 4)]
                    p.op("PE", grp, reads=kin + ["identbf"], writes=[f"PB0.{half}"])
                    p.op("DVE", CP(dsts[:, 128 * ccb: 128 * (ccb + 4)],
                                   ps["PB0"].bitcast(BF16)[:, 1024 * (ccb // 4): 1024 * (ccb // 4) + 512]),
                         reads=[f"PB0.{half}"], writes=kouts)

        def stage_b(c):
            bi = c % 2
            p.op("DVE", CP(v.Sbf, sb["Sall"]), reads=["Sall"], writes=v.keys("Sbf"))

            def head_S(hh):
                cc = hh // 2
                rb = 64 * (hh % 2)
                qs_h = QSFM[bi][rb:rb + 64, 128 * cc: 128 * (cc + 1)]
                ks_h = KSFM[bi][rb:rb + 64, 128 * cc: 128 * (cc + 1)]
                pat = f"PS{2 + (hh % 2)}"
                p.op("PE", MM(ps[pat][:, 0:128], ks_h, qs_h, True, True),
                     reads=kKSFM[bi] + kQSFM[bi], writes=[pat])
                atp_h = v.atp[:, 128 * (hh % 8): 128 * (hh % 8 + 1)]
                p.op("DVE", TT(atp_h, ps[pat][:, 0:128], sb["tribf"], ALU.mult),
                     reads=[pat, "tribf"], writes=[f"atpS{hh % 8}"] + v.keys("atp"))

            head_S(0)
            for hh in range(16):
                cc = hh // 2
                rb = 64 * (hh % 2)
                qs_h = QSFM[bi][rb:rb + 64, 128 * cc: 128 * (cc + 1)]
                srow, scol = 64 * (hh % 2), 128 * (hh // 2)
                sbf_h = v.Sbf[srow:srow + 64, scol:scol + 65]
                atp_h = v.atp[:, 128 * (hh % 8): 128 * (hh % 8 + 1)]
                p.op("PE", [
                    MM(ps["PB1"][srow:srow + 64, scol:scol + 64],
                       KCBF[bi][:, 64 * hh: 64 * (hh + 1)], VBF[bi][:, 64 * hh: 64 * (hh + 1)],
                       True, False),
                    MM(ps["PB1"][srow:srow + 64, scol + 64:scol + 65],
                       KCBF[bi][:, 64 * hh: 64 * (hh + 1)], sb["onesbf"], False, True),
                ], reads=kKCBF[bi] + kVBF[bi] + ["onesbf"], writes=["PB1"])
                if hh + 1 < 16:
                    head_S(hh + 1)
                pnum = f"PS{hh % 2}"
                p.op("PE", [
                    MM(ps[pnum][:, 0:64], atp_h, VBF[bi][:, 64 * hh: 64 * (hh + 1)], True, False),
                    MM(ps[pnum][:, 64:65], atp_h, sb["onesbf"], False, False),
                    MM(ps[pnum][:, 0:65], qs_h, sbf_h, False, True),
                ], reads=[f"atpS{hh % 8}"] + kVBF[bi] + ["onesbf"] + v.keys("Sbf") + kQSFM[bi],
                    writes=[pnum, "atpRD"])
                ch0, kh0 = (c0, "col0") if hh % 2 == 0 else (sb["col2"], "col2")
                ch1, kh1 = (c1, "col1") if hh % 2 == 0 else (sb["col3"], "col3")
                p.op("ACT", ACTV(ch0, ps[pnum][:, 64:65], AF.Relu, bias=sb["epscol"]),
                     reads=[pnum, "epscol"], writes=[kh0])
                p.op("DVE", RECIP(ch1, ch0), reads=[kh0], writes=[kh1])
                p.op("ACT", ACTV(v.attn_t[:, 64 * hh: 64 * (hh + 1)], ps[pnum][:, 0:64],
                                 AF.Identity, scale=ch1), reads=[pnum, kh1], writes=v.keys("attn_t"))
            cS = CUMCS[bi].unsqueeze(2).broadcast_to([128, 8, 128])
            p.op("DVE", TT(sb["Sall"].rearrange("p (q c) -> p q c", c=128),
                           sb["Sall"].rearrange("p (q c) -> p q c", c=128), cS, ALU.mult),
                 reads=["Sall", f"cumcS{bi}"], writes=["Sall"])
            p.op("DVE", TT(sb["Sall"], sb["Sall"], ps["PB1"], ALU.add),
                 reads=["Sall", "PB1"], writes=["Sall"])
            for ccb in (0, 4):
                grp = [TRANS(ps[f"PS{2 + ccb // 4}"].bitcast(BF16)[:, 128 * (cc - ccb): 128 * (cc - ccb + 1)],
                             v.attn_t[:, 128 * cc: 128 * (cc + 1)], sb["identbf"])
                       for cc in range(ccb, ccb + 4)]
                p.op("PE", grp, reads=v.keys("attn_t") + ["identbf"], writes=[f"PS{2 + ccb // 4}"])
                p.op("ACT", ACP(v.attnfm[:, 128 * ccb: 128 * (ccb + 4)],
                                ps[f"PS{2 + ccb // 4}"].bitcast(BF16)[:, 0:512]),
                     reads=[f"PS{2 + ccb // 4}"], writes=v.keys("attnfm"))
            for dh in range(2):
                grp = [MM(ps[f"PS{dh}"], v.attnfm[:, 128 * cc: 128 * (cc + 1)],
                          sb["WHM"][:, 8192 * 3 + 1024 * cc + 512 * dh:
                                    8192 * 3 + 1024 * cc + 512 * (dh + 1)],
                          cc == 0, cc == 7) for cc in range(8)]
                p.op("PE", grp, reads=v.keys("attnfm") + K("WHM", 8192 * 3, 8192 * 4),
                     writes=[f"PS{dh}"])
            xt = sb["SCR"][:, 0:1024]
            kx = K("SCR", 0, 1024)
            p.dma("SP", [DMA(xt, hb["x"][t0 + 128 * c: t0 + 128 * (c + 1), :])], writes=kx)
            mu = sb["stats"][:, 2 * c: 2 * c + 1]
            rs = sb["stats"][:, 2 * c + 1: 2 * c + 2]
            p.op("DVE", TS(tA, xt, mu, rs, ALU.subtract, ALU.mult),
                 reads=kx + [f"st.{t0 // 128 + c}"], writes=kA)
            for dh in range(2):
                p.op("DVE", TT(ht_[:, 512 * dh: 512 * (dh + 1)], ps[f"PS{dh}"],
                               v.boBs[:, 512 * dh: 512 * (dh + 1)], ALU.add),
                     reads=[f"PS{dh}"] + v.keys("boBs"), writes=kh)
            p.op("DVE", TT(ht_, ht_, tA, ALU.add), reads=kh + kA, writes=kh)
            c2, c3 = sb["col2"], sb["col3"]
            p.op("ACT", ACTV(tB, ht_, AF.Square, accum_out=c2), reads=kh, writes=kB + ["col2"])
            p.op("DVE", TS(c2, c2, 1.0 / 1024.0, EPS, ALU.mult, ALU.add), reads=["col2"], writes=["col2"])
            p.op("ACT", ACTV(c2, c2, AF.Ln), reads=["col2"], writes=["col2"])
            p.op("ACT", ACTV(c3, c2, AF.Exp, scale=-0.5), reads=["col2"], writes=["col3"])
            s0t = sb["XF32"][:, 1024 * c: 1024 * (c + 1)]
            p.op("DVE", TS1(tB, ht_, c3, ALU.mult), reads=kh + ["col3"], writes=kB)
            p.op("DVE", TT(s0t, tB, v.scaleBs, ALU.mult),
                 reads=kB + v.keys("scaleBs"), writes=K("XF32", 1024 * c, 1024 * (c + 1)))

        stage_a(0)
        if NCH > 1:
            stage_a(1)
        for c in range(NCH):
            stage_b(c)
            if c + 2 < NCH:
                stage_a(c + 2)

        for est in range(32):
            fb = est % 2
            if est == 0:
                p.dma("POOL", [DMA(sb["SCR"].bitcast(BF16)[:, 0:16384].rearrange("p (c d) -> p c d", c=32),
                                   hb["F2"][:, 0:512].rearrange("(c p) d -> p c d", p=128))],
                      writes=[k for b in range(8) for k in K("SCR", 1024 * b, 1024 * (b + 1))])
            p.dma("POOL", [
                DMA(v.f1buf[:, 1024 * fb: 1024 * (fb + 1)].rearrange("p (c e) -> p c e", c=8),
                    hb["F1"][:, 128 * est: 128 * (est + 1)].rearrange("(c p) e -> p c e", p=128)),
            ], writes=v.keys("f1buf", 1024 * fb, 1024 * (fb + 1)))
            for tch in range(SC // 512):
                grp = [MM(ps["PB0"][:, 0:512],
                          v.f1buf[:, 1024 * fb + 128 * mc: 1024 * fb + 128 * (mc + 1)],
                          v.xnfmbf[:, SC * mc + 512 * tch: SC * mc + 512 * (tch + 1)],
                          mc == 0, False) for mc in range(8)]
                grp.append(MM(ps["PB0"][:, 0:512], v.replbf[:, 128 * est: 128 * (est + 1)],
                              v.m1bf[:, 512 * tch: 512 * (tch + 1)], False, True))
                p.op("PE", grp, reads=v.keys("f1buf", 1024 * fb, 1024 * (fb + 1))
                     + v.keys("replbf") + [*v.keys("m1bf")] + xnall, writes=["PB0.a"])
                hmdst = sb["WHM"][:, SC * est + 512 * tch: SC * est + 512 * (tch + 1)]
                bias_c = sb["fbcolss"][:, est:est + 1]
                if (est + tch) % 2 == 0:
                    p.op("DVE", TS(hmdst, ps["PB0"][:, 0:512], bias_c, 0.0, ALU.add, ALU.max),
                         reads=["PB0.a", "fbcolss"], writes=K("WHM", SC * est, SC * (est + 1)))
                else:
                    p.op("ACT", ACTV(hmdst, ps["PB0"][:, 0:512], AF.Relu, bias=bias_c),
                         reads=["PB0.a", "fbcolss"], writes=K("WHM", SC * est, SC * (est + 1)))
        if stop == "ffn1":
            continue
        scr_bf = sb["SCR"].bitcast(BF16)
        whmall = [k for est in range(32) for k in K("WHM", SC * est, SC * (est + 1))]
        scrF2 = [k for b in range(8) for k in K("SCR", 1024 * b, 1024 * (b + 1))]
        sl2 = sb["SL2"]
        slots = []
        for nmf in ("kcbf", "vbf", "qsfm", "ksfm", "atp", "attn_t", "attnfm",
                    "f1buf", "f2buf", "Sbf"):
            lo, hi = v.off[nmf]
            slots += [o for o in range(lo, hi - 511, 512)]
        need = 32 - len(slots)
        xlo, xhi = v.off["xnfmbf"]
        slots += [o for o in range(max(xlo, xhi - 512 * need), xhi - 511, 512)]
        slots = slots[:32]
        slotkeys = sorted({k for o in slots for k in K("SL2", o, o + 512)})
        f2src = hb["F2"][:, 512:1024].rearrange("(c p) d -> p c d", p=128)
        for hbk in range(2):
            p.dma("POOL", [DMA(sl2[:, slots[est]: slots[est] + 512], f2src[:, est])
                           for est in range(16 * hbk, 16 * hbk + 16)],
                  reads=["atpRD"],
                  writes=sorted({k for o in slots[16 * hbk: 16 * hbk + 16]
                                 for k in K("SL2", o, o + 512)}))
        for dh in range(2):
            for ti in range(NCH):
                pacc = f"PS{ti % 2}"
                if dh == 0:
                    grp = [MM(ps[pacc], sb["WHM"][:, SC * est + 128 * ti: SC * est + 128 * (ti + 1)],
                              scr_bf[:, 512 * est: 512 * (est + 1)], est == 0, est == 31)
                           for est in range(32)]
                    p.op("PE", grp, reads=whmall + scrF2, writes=[pacc])
                else:
                    grp = [MM(ps[pacc], sb["WHM"][:, SC * est + 128 * ti: SC * est + 128 * (ti + 1)],
                              sl2[:, slots[est]: slots[est] + 512], est == 0, est == 31)
                           for est in range(32)]
                    p.op("PE", grp, reads=whmall + slotkeys, writes=[pacc])
                gt = v.qsbf.bitcast(F32)[:, 0:512]
                xt2 = v.ksbf.bitcast(F32)[:, 0:512]
                kg, kx2 = v.keys("qsbf"), v.keys("ksbf")
                p.op("ACT", ACTV(gt, ps[pacc], AF.Silu), reads=[pacc], writes=kg)
                p.dma("SP", [DMA(xt2, hb["x"][t0 + 128 * ti: t0 + 128 * (ti + 1),
                                              512 * dh: 512 * (dh + 1)])], writes=kx2)
                s0h = sb["XF32"][:, 1024 * ti + 512 * dh: 1024 * ti + 512 * (dh + 1)]
                p.op("DVE", TT(gt, gt, s0h, ALU.mult),
                     reads=kg + K("XF32", 1024 * ti, 1024 * (ti + 1)), writes=kg)
                p.op("DVE", TT(gt, gt, xt2, ALU.add), reads=kg + kx2, writes=kg)
                p.dma("ACT", [DMA(hb["x1_h"][t0 + 128 * ti: t0 + 128 * (ti + 1),
                                             512 * dh: 512 * (dh + 1)], gt)],
                      reads=kg, writes=[f"x1.{(t0 // 128 + ti) * 2 + dh}"])


def build_program(T, parts=("l0", "l1", "moe", "final"), scratch_kinds=None, l0_stop=None, l1_stop=None, SC=None):
    nc = bass.Bass("TRN2", target_bir_lowering=False)
    cx = Ctx(nc, T, SC)
    declare_io(cx)
    sk = dict(scratch_kinds or {})
    for n in ("x1_h", "s1_h", "xa_h"):
        cx.dram(n, (T, 1024), F32, sk.get(n, "Internal"))
    cx.dram("y_h", (T, 1024), BF16, sk.get("y_h", "Internal"))
    for n in ("qr_h",):
        cx.dram(n, (1024, T), BF16, sk.get(n, "Internal"))
    cx.dram("rms2t_h", (T, 1024), BF16, sk.get("rms2t_h", "Internal"))
    cx.dram("idx_h", (4 * (pad_for(T) + 1), 1), I32, sk.get("idx_h", "Internal"))
    if "final" in parts:
        cx.dram("out", (T, 1024), F32, "ExternalOutput")
    alloc_globals(cx)
    alloc_l0(cx)
    alloc_l1(cx)
    emit_load_consts(cx)
    if "l0" in parts:
        emit_l0(cx, stop=l0_stop)
    if "l1" in parts:
        emit_l1(cx, stop=l1_stop)
    if "moe" in parts:
        emit_moe(cx)
    if "final" in parts:
        emit_final(cx)
    cx.p.wait_all_done()
    cx.p.finalize()
    return nc, cx


def make_in_map(x_slice, host, T):
    m = {"x": np.ascontiguousarray(x_slice)}
    m.update(host)
    return m


def alloc_l1(cx):
    T = cx.T
    NT = T // 128
    cx.sbuf("stats1", (128, 2 * NT), F32)
    cx.sbuf("gbBs", (128, 4), F32)
    cx.sbuf("hdmi", (128, 1), I32)
    cx.sbuf("hdmf", (128, 1), F32)


def emit_l1(cx, stop=None):
    p, hb, sb, ps = cx.p, cx.hb, cx.sb, cx.ps
    T = cx.T
    SC = cx.SC
    NT = T // 128
    NSUP = NT // 4
    sl2 = sb["SL2"]
    scr = sb["SCR"]
    scrb = scr.bitcast(BF16)
    xf = sb["XF32"]
    xfb = xf.bitcast(BF16)

    def K(slab, lo, hi):
        return [f"{slab}.{b}" for b in range(lo // 1024, (hi + 1023) // 1024)]

    def KB(slab, lo, hi):
        return K(slab, lo // 2, hi // 2)

    def SK(lo, hi):
        return [f"SL2.{b}" for b in range(lo // 1024, (hi + 1023) // 1024)]

    c0, c1, c2, c3 = sb["col0"], sb["col1"], sb["col2"], sb["col3"]

    xt = scr[:, 0:1024]
    xn = scr[:, 1024:2048]
    sq = scr[:, 2048:3072]
    kx, kn, ksq = K("SCR", 0, 1024), K("SCR", 1024, 2048), K("SCR", 2048, 3072)
    whmK = lambda cc: K("WHM", T * cc, T * (cc + 1))
    for ti in range(NT):
        p.dma("SP", [DMA(xt, hb["x1_h"][128 * ti: 128 * (ti + 1), :])],
              reads=[f"x1.{2 * ti}", f"x1.{2 * ti + 1}"], writes=kx)
        mu = sb["stats1"][:, 2 * ti: 2 * ti + 1]
        rs = sb["stats1"][:, 2 * ti + 1: 2 * ti + 2]
        p.op("DVE", RSUM(c0, xt), reads=kx, writes=["col0"])
        p.op("ACT", ACTV(sq, xt, AF.Square, accum_out=c1), reads=kx, writes=ksq + ["col1"])
        p.op("DVE", TS1(mu, c0, 1.0 / 1024.0, ALU.mult), reads=["col0"], writes=["stats1"])
        p.op("DVE", TT(c0, mu, mu, ALU.mult), reads=["stats1"], writes=["col0"])
        p.op("DVE", TS(c1, c1, 1.0 / 1024.0, 1e-5, ALU.mult, ALU.add), reads=["col1"], writes=["col1"])
        p.op("DVE", TT(c1, c1, c0, ALU.subtract), reads=["col0", "col1"], writes=["col1"])
        p.op("ACT", ACTV(c1, c1, AF.Ln), reads=["col1"], writes=["col1"])
        p.op("ACT", ACTV(rs, c1, AF.Exp, scale=-0.5), reads=["col1"], writes=["stats1"])
        p.op("DVE", TS(xn, xt, mu, rs, ALU.subtract, ALU.mult), reads=kx + ["stats1"], writes=kn)
        for ccb in (0, 4):
            grp = [TRANS(ps["PB0"][:, 128 * (cc - ccb): 128 * (cc - ccb + 1)],
                         xn[:, 128 * cc: 128 * (cc + 1)], sb["ident"])
                   for cc in range(ccb, ccb + 4)]
            p.op("PE", grp, reads=kn + ["ident"], writes=["PB0.a"])
            for cc in range(ccb, ccb + 4):
                srcp = ps["PB0"][:, 128 * (cc - ccb): 128 * (cc - ccb + 1)]
                p.op("ACT", ACP(sb["WHM"][:, T * cc + 128 * ti: T * cc + 128 * (ti + 1)], srcp),
                     reads=["PB0.a"], writes=whmK(cc))
    xn1all = [k for cc in range(8) for k in whmK(cc)]
    if stop == "ln1":
        return

    q1T = sl2[:, 0:T]
    k1T = sl2[:, T:2 * T]
    v1a = sl2[:, 2 * T:2 * T + 132 * NT]
    base = 2 * T + 132 * NT
    womb = sl2[:, base:base + 1024]
    wqb = sl2[:, base + 1024:base + 2048]
    wkb = sl2[:, base + 2048:base + 3072]
    wvb = sl2[:, base + 3072:base + 4096]
    abf = sl2[:, base + 4096:base + 4608]
    wo1f = sl2.bitcast(F32)[:, (base + 4608) // 2: (base + 4608) // 2 + 1024]
    kq1, kk1 = SK(0, T), SK(T, 2 * T)
    kv1 = SK(2 * T, 2 * T + 132 * NT)
    kwom, kwq, kwk, kwv = (SK(base, base + 1024), SK(base + 1024, base + 2048),
                           SK(base + 2048, base + 3072), SK(base + 3072, base + 4096))
    kabf = SK(base + 4096, base + 4608)
    kwo1 = SK(base + 4608, base + 6656)
    p.dma("POOL", [
        DMA(wqb.rearrange("p (c d) -> p c d", c=8), hb["w_q1"].rearrange("(c p) d -> p c d", p=128)),
        DMA(wkb.rearrange("p (c d) -> p c d", c=8), hb["w_k1"].rearrange("(c p) d -> p c d", p=128)),
        DMA(wvb.rearrange("p (c d) -> p c d", c=8), hb["w_v1"].rearrange("(c p) d -> p c d", p=128)),
    ], writes=kwq + kwk + kwv)
    p.dma("SP", [DMA(wo1f, hb["w_o1"][:]), DMA(sb["hdmi"], hb["hdm"][:])],
          writes=kwo1 + ["hdmi"])
    p.op("DVE", CP(sb["hdmf"], sb["hdmi"]), reads=["hdmi"], writes=["hdmf"])
    p.op("DVE", TS1(womb, wo1f, sb["hdmf"], ALU.mult), reads=kwo1 + ["hdmf"], writes=kwom)
    for tch in range(T // 512):
        for wj, (which, wb, dst, kdst) in enumerate((("q", wqb, q1T, kq1), ("k", wkb, k1T, kk1))):
            pk = "PS0" if (2 * tch + wj) % 2 == 0 else "PS2"
            grp = [MM(ps[pk], wb[:, 128 * mc: 128 * (mc + 1)],
                      sb["WHM"][:, T * mc + 512 * tch: T * mc + 512 * (tch + 1)],
                      mc == 0, mc == 7) for mc in range(8)]
            p.op("PE", grp, reads=xn1all + (kwq if which == "q" else kwk), writes=[pk])
            p.op("DVE", CP(dst[:, 512 * tch: 512 * (tch + 1)], ps[pk]),
                 reads=[pk], writes=kdst)
    for ti in range(NT):
        pk = "PS1" if ti % 2 == 0 else "PS3"
        grp = [MM(ps[pk][:, 0:128], sb["WHM"][:, T * mc + 128 * ti: T * mc + 128 * (ti + 1)],
                  wvb[:, 128 * mc: 128 * (mc + 1)], mc == 0, mc == 7) for mc in range(8)]
        p.op("PE", grp, reads=xn1all + kwv, writes=[pk])
        p.op("DVE", CP(v1a[:, 132 * ti: 132 * ti + 128], ps[pk][:, 0:128]),
             reads=[pk], writes=kv1)
    p.op("DVE", MEMSET(v1a.rearrange("p (t c) -> p t c", c=132)[:, :, 128:129], 1.0),
         reads=[], writes=kv1)
    abfb = sl2[:, base + 4608 + 2048:base + 4608 + 2048 + 512]
    kabfb = SK(base + 6656, base + 7168)
    for isup in range(NSUP):
        def st1(jt, isup=isup):
            half = jt % 2
            p.op("PE", MM(ps["PB0"][:, 512 * half: 512 * (half + 1)],
                          k1T[:, 128 * jt: 128 * (jt + 1)],
                          q1T[:, 512 * isup: 512 * (isup + 1)], True, True),
                 reads=kq1 + kk1, writes=[f"PB0.{'ab'[half]}"])

        def exp1(jt):
            half = jt % 2
            ab = abf if half == 0 else abfb
            kab = kabf if half == 0 else kabfb
            p.op("ACT", ACTV(ab, ps["PB0"][:, 512 * half: 512 * (half + 1)], AF.Exp,
                             bias=sb["c_m4"], scale=float(1.0 / np.sqrt(128.0))),
                 reads=[f"PB0.{'ab'[half]}", "c_m4"], writes=kab)

        def num1(jt):
            half = jt % 2
            ab = abf if half == 0 else abfb
            kab = kabf if half == 0 else kabfb
            for s in range(4):
                p.op("PE", MM(ps[f"PS{s}"][:, 0:129], ab[:, 128 * s: 128 * (s + 1)],
                              v1a[:, 132 * jt: 132 * jt + 129], jt == 0, jt == NT - 1),
                     reads=kab + kv1, writes=[f"PS{s}"])

        st1(0)
        exp1(0)
        for jt in range(NT):
            if jt + 1 < NT:
                st1(jt + 1)
                exp1(jt + 1)
            num1(jt)
        for s in range(4):
            ti = 4 * isup + s
            p.op("DVE", RECIP(c0, ps[f"PS{s}"][:, 128:129]), reads=[f"PS{s}"], writes=["col0"])
            ot = scrb[:, 0:128]
            p.op("DVE", TS1(ot, ps[f"PS{s}"][:, 0:128], c0, ALU.mult),
                 reads=[f"PS{s}", "col0"], writes=K("SCR", 0, 1024))
            p.op("PE", TRANS(ps["PB0"].bitcast(BF16)[:, 0:128], ot, sb["identbf"]),
                 reads=K("SCR", 0, 1024) + ["identbf"], writes=["PB0.a"])
            otT = scrb[:, 128:256]
            p.op("DVE", CP(otT, ps["PB0"].bitcast(BF16)[:, 0:128]),
                 reads=["PB0.a"], writes=K("SCR", 0, 1024))
            for dh in range(2):
                p.op("PE", MM(ps["PB0"][:, 0:512], otT, womb[:, 512 * dh: 512 * (dh + 1)], True, True),
                     reads=K("SCR", 0, 1024) + kwom, writes=["PB0.a"])
                yt = scr[:, 1024:1536]
                p.op("DVE", CP(yt, ps["PB0"][:, 0:512]), reads=["PB0.a"], writes=K("SCR", 1024, 2048))
                p.dma("ACT", [DMA(hb["s1_h"][128 * ti: 128 * (ti + 1), 512 * dh: 512 * (dh + 1)], yt)],
                      reads=K("SCR", 1024, 2048), writes=[f"s1.{ti}"])

    if stop == "attn":
        return
    wqmb = sl2[:, 0:8192]
    wdkvb = sl2[:, 8192:12288]
    kwqm, kwdkv = SK(0, 8192), SK(8192, 12288)
    p.dma("POOL", [
        DMA(wqmb.rearrange("p (c d) -> p c d", c=8), hb["wqmT"].rearrange("(c p) d -> p c d", p=128)),
        DMA(wdkvb.rearrange("p (c d) -> p c d", c=8), hb["wdkvT"].rearrange("(c p) d -> p c d", p=128)),
    ], writes=kwqm + kwdkv)
    n1w = sl2.bitcast(F32)[:, 10240:11264]
    kn1w = SK(20480, 22528)
    p.dma("SP", [DMA(n1w, hb["n1wB"][:])], writes=kn1w)
    rm = scr[:, 1024:2048]
    rmb = scrb[:, 4096:5120]
    rmf = scrb[:, 5120:6144]
    qtm = scr[:, 3072:4096]
    tA = scr[:, 4096:5120]
    tB = scr[:, 5120:6144]
    krm, krmb, krmf = K("SCR", 1024, 2048), K("SCR", 2048, 2560), K("SCR", 2560, 3072)
    kqtm, kA, kB = K("SCR", 3072, 4096), K("SCR", 4096, 5120), K("SCR", 5120, 6144)
    latfm = xfb
    klat = lambda lc: KB("XF32", lc * T, (lc + 1) * T)
    latall = [k for lc in range(4) for k in klat(lc)]
    for ti in range(NT):
        p.dma("SP", [DMA(xt, hb["x1_h"][128 * ti: 128 * (ti + 1), :])],
              reads=[f"x1.{2 * ti}", f"x1.{2 * ti + 1}"], writes=kx)
        mu = sb["stats1"][:, 2 * ti: 2 * ti + 1]
        rs = sb["stats1"][:, 2 * ti + 1: 2 * ti + 2]
        p.op("DVE", TS(xn, xt, mu, rs, ALU.subtract, ALU.mult), reads=kx + ["stats1"], writes=kn)
        p.op("ACT", ACTV(sq, xn, AF.Square, accum_out=c1), reads=kn, writes=ksq + ["col1"])
        p.op("DVE", TS(c1, c1, 1.0 / 1024.0, 1e-6, ALU.mult, ALU.add), reads=["col1"], writes=["col1"])
        p.op("ACT", ACTV(c1, c1, AF.Ln), reads=["col1"], writes=["col1"])
        p.op("ACT", ACTV(c0, c1, AF.Exp, scale=-0.5), reads=["col1"], writes=["col0"])
        p.op("DVE", TS1(rm, xn, c0, ALU.mult), reads=kn + ["col0"], writes=krm)
        p.op("DVE", TT(rm, rm, n1w, ALU.mult), reads=krm + kn1w, writes=krm)
        p.op("DVE", CP(rmb, rm), reads=krm, writes=krmb)
        for ccb in (0, 4):
            grp = [TRANS(ps["PB0"].bitcast(BF16)[:, 128 * (cc - ccb): 128 * (cc - ccb + 1)],
                         rmb[:, 128 * cc: 128 * (cc + 1)], sb["identbf"])
                   for cc in range(ccb, ccb + 4)]
            p.op("PE", grp, reads=krmb + ["identbf"], writes=["PB0.a"])
            p.op("ACT", ACP(rmf[:, 128 * ccb: 128 * (ccb + 4)],
                            ps["PB0"].bitcast(BF16)[:, 0:512]),
                 reads=["PB0.a"], writes=krmf)
        for dh in range(2):
            grp = [MM(ps[f"PS{dh}"], rmf[:, 128 * mc: 128 * (mc + 1)],
                      wqmb[:, 1024 * mc + 512 * dh: 1024 * mc + 512 * (dh + 1)],
                      mc == 0, mc == 7) for mc in range(8)]
            p.op("PE", grp, reads=krmf + kwqm, writes=[f"PS{dh}"])
            p.op("ACT", ACP(qtm[:, 512 * dh: 512 * (dh + 1)], ps[f"PS{dh}"]),
                 reads=[f"PS{dh}"], writes=kqtm)
        nslot = min(NT, SC // 128)
        if ti % nslot == 0:
            p.dma("SP", [DMA(sb["cost"][:, 0:32 * nslot], hb["COSt"][:, 32 * ti: 32 * (ti + nslot)]),
                         DMA(sb["sint"][:, 0:32 * nslot], hb["SINt"][:, 32 * ti: 32 * (ti + nslot)])],
                  writes=["cost", "sint"])
        cosv = sb["cost"][:, 32 * (ti % nslot): 32 * (ti % nslot) + 32].unsqueeze(1).broadcast_to([128, 16, 32])
        sinv = sb["sint"][:, 32 * (ti % nslot): 32 * (ti % nslot) + 32].unsqueeze(1).broadcast_to([128, 16, 32])
        s3 = qtm.rearrange("p (h d) -> p h d", d=64)
        a3 = tA.rearrange("p (h d) -> p h d", d=64)
        b3 = tB.rearrange("p (h d) -> p h d", d=64)
        p.op("DVE", TT(a3[:, :, 0:32], s3[:, :, 0:32], cosv, ALU.mult), reads=kqtm + ["cost"], writes=kA)
        p.op("DVE", TT(a3[:, :, 32:64], s3[:, :, 32:64], sinv, ALU.mult), reads=kqtm + ["sint"], writes=kA)
        p.op("DVE", TT(b3[:, :, 0:32], s3[:, :, 32:64], cosv, ALU.mult), reads=kqtm + ["cost"], writes=kB)
        p.op("DVE", TT(b3[:, :, 32:64], s3[:, :, 0:32], sinv, ALU.mult), reads=kqtm + ["sint"], writes=kB)
        p.op("DVE", TT(s3[:, :, 0:32], a3[:, :, 0:32], a3[:, :, 32:64], ALU.subtract), reads=kA, writes=kqtm)
        p.op("DVE", TT(s3[:, :, 32:64], b3[:, :, 0:32], b3[:, :, 32:64], ALU.add), reads=kB, writes=kqtm)
        qbf = scrb[:, 4096 + 2048:4096 + 3072]
        qbf = scrb[:, 12288:13312]
        kqbf = K("SCR", 6144, 6656 + 0) if False else K("SCR", 6144, 6656)
        kqbf = KB("SCR", 12288, 13312)
        p.op("DVE", CP(qbf, qtm), reads=kqtm, writes=kqbf)
        for ccb in (0, 4):
            grp = [TRANS(ps["PB0"].bitcast(BF16)[:, 128 * (cc - ccb): 128 * (cc - ccb + 1)],
                         qbf[:, 128 * cc: 128 * (cc + 1)], sb["identbf"])
                   for cc in range(ccb, ccb + 4)]
            p.op("PE", grp, reads=kqbf + ["identbf"], writes=["PB0.a"])
            p.op("ACT", ACP(scrb[:, 13312 + 128 * ccb: 13312 + 128 * (ccb + 4)],
                            ps["PB0"].bitcast(BF16)[:, 0:512]),
                 reads=["PB0.a"], writes=KB("SCR", 13312, 14336))
        p.dma("ACT", [DMA(hb["qr_h"].rearrange("(c p) t -> p c t", p=128)[:, :, 128 * ti: 128 * (ti + 1)],
                          scrb[:, 13312:14336].rearrange("p (c t) -> p c t", c=8))],
              reads=KB("SCR", 13312, 14336), writes=[f"qr.{ti}"])
        grp = [MM(ps["PS2"], rmf[:, 128 * mc: 128 * (mc + 1)],
                  wdkvb[:, 512 * mc: 512 * (mc + 1)], mc == 0, mc == 7) for mc in range(8)]
        p.op("PE", grp, reads=krmf + kwdkv, writes=["PS2"])
        lat = scr[:, 3072:3584]
        klatt = K("SCR", 3072, 4096)
        p.op("DVE", RSUM(c0, ps["PS2"]), reads=["PS2"], writes=["col0"])
        p.op("ACT", ACTV(lat, ps["PS2"], AF.Square, accum_out=c1), reads=["PS2"], writes=klatt + ["col1"])
        p.op("DVE", TS1(c0, c0, 1.0 / 512.0, ALU.mult), reads=["col0"], writes=["col0"])
        p.op("DVE", TT(c2, c0, c0, ALU.mult), reads=["col0"], writes=["col2"])
        p.op("DVE", TS(c1, c1, 1.0 / 512.0, 1e-5, ALU.mult, ALU.add), reads=["col1"], writes=["col1"])
        p.op("DVE", TT(c1, c1, c2, ALU.subtract), reads=["col1", "col2"], writes=["col1"])
        p.op("ACT", ACTV(c1, c1, AF.Ln), reads=["col1"], writes=["col1"])
        p.op("ACT", ACTV(c2, c1, AF.Exp, scale=-0.5), reads=["col1"], writes=["col2"])
        p.op("DVE", TS(lat, ps["PS2"], c0, c2, ALU.subtract, ALU.mult),
             reads=["PS2", "col0", "col2"], writes=klatt)
        lngv = sl2.bitcast(F32)[:, 11264:11776]
        lnbv = sl2.bitcast(F32)[:, 11776:12288]
        klng, klnb = SK(22528, 23552), SK(23552, 24576)
        if ti == 0:
            p.dma("SP", [DMA(lngv, hb["lngB"][:]), DMA(lnbv, hb["lnbB"][:])], writes=klng + klnb)
        p.op("DVE", TT(lat, lat, lngv, ALU.mult), reads=klatt + klng, writes=klatt)
        p.op("DVE", TT(lat, lat, lnbv, ALU.add), reads=klatt + klnb, writes=klatt)
        latb = scrb[:, 7168:7680]
        klatb = K("SCR", 3584, 4096)
        p.op("DVE", CP(latb, lat), reads=klatt, writes=klatb)
        grp = [TRANS(ps["PB0"].bitcast(BF16)[:, 128 * lc: 128 * (lc + 1)],
                     latb[:, 128 * lc: 128 * (lc + 1)], sb["identbf"]) for lc in range(4)]
        p.op("PE", grp, reads=klatb + ["identbf"], writes=["PB0.a"])
        for lc in range(4):
            p.op("ACT", ACP(latfm[:, T * lc + 128 * ti: T * lc + 128 * (ti + 1)],
                            ps["PB0"].bitcast(BF16)[:, 128 * lc: 128 * (lc + 1)]),
                 reads=["PB0.a"], writes=klat(lc))

    if stop == "proj":
        return
    wukb = scrb[:, 0:4096]
    wuvb = scrb[:, 12288:16384]
    kwuk = KB("SCR", 0, 4096)
    kwuv = KB("SCR", 12288, 16384)
    p.dma("POOL", [
        DMA(wukb.rearrange("p (c d) -> p c d", c=4), hb["wukT"].rearrange("(c p) d -> p c d", p=128)),
        DMA(wuvb.rearrange("p (c d) -> p c d", c=4), hb["wuvT"].rearrange("(c p) d -> p c d", p=128)),
    ], writes=kwuk + kwuv)
    v1m = sl2[:, 0:1040 * NT]
    kv1m = lambda jt: SK(1040 * jt, 1040 * (jt + 1))
    ktm = scr[:, 3072:4096]
    kktm = K("SCR", 3072, 4096)
    for ti in range(NT):
        for dh in range(2):
            grp = [MM(ps[f"PS{dh}"], latfm[:, T * lc + 128 * ti: T * lc + 128 * (ti + 1)],
                      wukb[:, 1024 * lc + 512 * dh: 1024 * lc + 512 * (dh + 1)],
                      lc == 0, lc == 3) for lc in range(4)]
            p.op("PE", grp, reads=latall + kwuk, writes=[f"PS{dh}"])
            p.op("ACT", ACP(ktm[:, 512 * dh: 512 * (dh + 1)], ps[f"PS{dh}"]),
                 reads=[f"PS{dh}"], writes=kktm)
        nslot = min(NT, SC // 128)
        if ti % nslot == 0:
            p.dma("SP", [DMA(sb["cost"][:, 0:32 * nslot], hb["COSt"][:, 32 * ti: 32 * (ti + nslot)]),
                         DMA(sb["sint"][:, 0:32 * nslot], hb["SINt"][:, 32 * ti: 32 * (ti + nslot)])],
                  writes=["cost", "sint"])
        cosv = sb["cost"][:, 32 * (ti % nslot): 32 * (ti % nslot) + 32].unsqueeze(1).broadcast_to([128, 16, 32])
        sinv = sb["sint"][:, 32 * (ti % nslot): 32 * (ti % nslot) + 32].unsqueeze(1).broadcast_to([128, 16, 32])
        s3 = ktm.rearrange("p (h d) -> p h d", d=64)
        a3 = tA.rearrange("p (h d) -> p h d", d=64)
        b3 = tB.rearrange("p (h d) -> p h d", d=64)
        p.op("DVE", TT(a3[:, :, 0:32], s3[:, :, 0:32], cosv, ALU.mult), reads=kktm + ["cost"], writes=kA)
        p.op("DVE", TT(a3[:, :, 32:64], s3[:, :, 32:64], sinv, ALU.mult), reads=kktm + ["sint"], writes=kA)
        p.op("DVE", TT(b3[:, :, 0:32], s3[:, :, 32:64], cosv, ALU.mult), reads=kktm + ["cost"], writes=kB)
        p.op("DVE", TT(b3[:, :, 32:64], s3[:, :, 0:32], sinv, ALU.mult), reads=kktm + ["sint"], writes=kB)
        p.op("DVE", TT(s3[:, :, 0:32], a3[:, :, 0:32], a3[:, :, 32:64], ALU.subtract), reads=kA, writes=kktm)
        p.op("DVE", TT(s3[:, :, 32:64], b3[:, :, 0:32], b3[:, :, 32:64], ALU.add), reads=kB, writes=kktm)
        kbf = scrb[:, 4096:5120]
        kkbf = KB("SCR", 4096, 5120)
        p.op("DVE", CP(kbf, ktm), reads=kktm, writes=kkbf)
        for ccb in (0, 4):
            grp = [TRANS(ps["PB0"].bitcast(BF16)[:, 128 * (cc - ccb): 128 * (cc - ccb + 1)],
                         kbf[:, 128 * cc: 128 * (cc + 1)], sb["identbf"])
                   for cc in range(ccb, ccb + 4)]
            p.op("PE", grp, reads=kkbf + ["identbf"], writes=["PB0.a"])
            for cc in range(ccb, ccb + 4):
                p.op("ACT", ACP(sb["WHM"][:, T * cc + 128 * ti: T * cc + 128 * (ti + 1)],
                                ps["PB0"].bitcast(BF16)[:, 128 * (cc - ccb): 128 * (cc - ccb + 1)]),
                     reads=["PB0.a"], writes=whmK(cc))
        for dh in range(2):
            grp = [MM(ps[f"PS{2 + dh}"], latfm[:, T * lc + 128 * ti: T * lc + 128 * (ti + 1)],
                      wuvb[:, 1024 * lc + 512 * dh: 1024 * lc + 512 * (dh + 1)],
                      lc == 0, lc == 3) for lc in range(4)]
            p.op("PE", grp, reads=latall + kwuv, writes=[f"PS{2 + dh}"])
            dstv = v1m[:, 1040 * ti + 520 * dh: 1040 * ti + 520 * (dh + 1)].rearrange(
                "p (h c) -> p h c", c=65)[:, :, 0:64]
            p.op("DVE", CP(dstv, ps[f"PS{2 + dh}"].rearrange("p (h c) -> p h c", c=64)),
                 reads=[f"PS{2 + dh}"], writes=kv1m(ti))
        p.op("DVE", MEMSET(v1m[:, 1040 * ti: 1040 * (ti + 1)].rearrange(
            "p (h c) -> p h c", c=65)[:, :, 64:65], 1.0), writes=kv1m(ti))

    if stop in ("kv", "kv_dump"):
        if stop == "kv_dump":
            if "rms2h" not in hb:
                cx.dram("rms2h", (1024, T), BF16, "ExternalOutput")
            p.dma("ACT", [DMA(hb["rms2h"].rearrange("(c p) t -> p c t", p=128),
                              sb["WHM"][:, 0:8 * T].rearrange("p (c t) -> p c t", c=8))],
                  reads=[k for cc in range(8) for k in whmK(cc)], writes=["rm2dump"])
        return
    qrb = scrb[:, 0:4096]
    kqrb = KB("SCR", 0, 4096)
    womtb = scrb[:, 4096:12288]
    kwomt = KB("SCR", 4096, 12288)
    attnb = scrb[:, 12288:16384]
    kattnb = KB("SCR", 12288, 16384)
    abf2 = xfb[:, 0:512]
    kabf2 = KB("XF32", 0, 1024)
    if 16 * (cx.SC) >= 12288:
        recB = xf[:, 4096:4608]
        recrow = xf[:, 4608:5120]
        krec = K("XF32", 4096, 5120)
        obf = xfb[0:64, 10240:10752]
        kobf = KB("XF32", 10240, 11264)
    else:
        b2_ = ((1040 * NT + 2047) // 2048) * 2048
        recB = sl2.bitcast(F32)[:, b2_ // 2: b2_ // 2 + 512]
        recrow = sl2.bitcast(F32)[:, b2_ // 2 + 512: b2_ // 2 + 1024]
        krec = SK(b2_, b2_ + 2048)
        obf = sl2[0:64, b2_ + 2048: b2_ + 2560]
        kobf = SK(b2_ + 2048, b2_ + 3072)
    p.dma("POOL", [DMA(womtb.rearrange("p (c d) -> p c d", c=8),
                       hb["womT"].rearrange("(c p) d -> p c d", p=128))], writes=kwomt)
    ABF = [[xfb[:, 0:512], xfb[:, 2048:2560]],
           [xfb[:, 4096:4608], xfb[:, 6144:6656]]]
    KABF = [[KB("XF32", 0, 1024), KB("XF32", 2048, 3072)],
            [KB("XF32", 4096, 5120), KB("XF32", 6144, 7168)]]
    for isup in range(NSUP):
        p.dma("SP", [DMA(qrb.rearrange("p (c t) -> p c t", c=8),
                         hb["qr_h"].rearrange("(c p) t -> p c t", p=128)[:, :, 512 * isup: 512 * (isup + 1)])],
              reads=[f"qr.{tt_}" for tt_ in range(4 * isup, 4 * isup + 4)], writes=kqrb)
        njj = 4 * isup + 4
        for qq in range(8):
            cc = qq

            STPS = [[(ps["PB0"][:, 0:512], "PB0.a"), (ps["PB0"][:, 512:1024], "PB0.b")],
                    [(ps["PS1"], "PS1"), (ps["PS3"], "PS3")]]

            def st_pair(jj, cc=cc, STPS=STPS):
                for half in range(2):
                    rb = 64 * half
                    stp, stk = STPS[jj % 2][half]
                    p.op("PE", MM(stp, sb["WHM"][rb:rb + 64, T * cc + 128 * jj: T * cc + 128 * (jj + 1)],
                                  qrb[rb:rb + 64, 512 * cc: 512 * (cc + 1)], True, True),
                         reads=whmK(cc) + kqrb, writes=[stk])

            def exp_pair(jj, isup=isup, STPS=STPS):
                dj = jj - 4 * isup
                for half in range(2):
                    ab = ABF[jj % 2][half]
                    kab = KABF[jj % 2][half]
                    stp, stk = STPS[jj % 2][half]
                    p.op("ACT", ACTV(ab, stp, AF.Exp, bias=sb["c_m4"], scale=0.125),
                         reads=[stk, "c_m4"], writes=kab)
                    if dj > 0:
                        p.op("DVE", MEMSET(ab[:, 0:128 * dj], 0.0), writes=kab)
                    if 0 <= dj <= 3:
                        p.op("DVE", TT(ab[:, 128 * dj: 128 * (dj + 1)],
                                       ab[:, 128 * dj: 128 * (dj + 1)], sb["tribf"], ALU.mult),
                             reads=kab + ["tribf"], writes=kab)

            def num_pair(jj, qq=qq, njj=njj):
                for half in range(2):
                    hh = 2 * qq + half
                    ab = ABF[jj % 2][half]
                    kab = KABF[jj % 2][half]
                    pnum = "PS0" if half == 0 else "PS2"
                    p.op("PE", MM(ps[pnum][0:65, 0:512],
                                  v1m[:, 1040 * jj + 65 * hh: 1040 * jj + 65 * (hh + 1)],
                                  ab, jj == 0, jj == njj - 1),
                         reads=kv1m(jj) + kab, writes=[pnum])

            st_pair(0)
            exp_pair(0)
            for jj in range(njj):
                if jj + 1 < njj:
                    st_pair(jj + 1)
                    exp_pair(jj + 1)
                num_pair(jj)
            for half in range(2):
                pnum = "PS0" if half == 0 else "PS2"
                p.op("DVE", RECIP(recrow[0:1, :], ps[pnum][64:65, 0:512]), reads=[pnum], writes=krec)
                p.op("PE", MM(ps["PS1"][0:64, 0:512], sb["onesrow"][:, 0:64], recrow[0:1, :], True, True),
                     reads=["onesrow"] + krec, writes=["PS1"])
                p.op("ACT", ACP(recB[0:64, :], ps["PS1"][0:64, 0:512]), reads=["PS1"], writes=krec)
                if half == 0:
                    dsta = attnb[0:64, 512 * cc: 512 * (cc + 1)]
                    p.op("DVE", TT(dsta, ps[pnum][0:64, 0:512], recB[0:64, :], ALU.mult),
                         reads=[pnum] + krec, writes=kattnb)
                else:
                    p.op("DVE", TT(obf, ps[pnum][0:64, 0:512], recB[0:64, :], ALU.mult),
                         reads=[pnum] + krec, writes=kobf)
                    p.dma("POOL", [DMA(attnb[64:128, 512 * cc: 512 * (cc + 1)], obf)],
                          reads=kobf, writes=kattnb)
        for s in range(4):
            ti = 4 * isup + s
            for dh in range(2):
                grp = [MM(ps[f"PS{2 + dh}"], attnb[:, 512 * cc2 + 128 * s: 512 * cc2 + 128 * (s + 1)],
                          womtb[:, 1024 * cc2 + 512 * dh: 1024 * cc2 + 512 * (dh + 1)],
                          cc2 == 0, cc2 == 7) for cc2 in range(8)]
                p.op("PE", grp, reads=kattnb + kwomt, writes=[f"PS{2 + dh}"])
            p.dma("SP", [DMA(xt, hb["x1_h"][128 * ti: 128 * (ti + 1), :])],
                  reads=[f"x1.{2 * ti}", f"x1.{2 * ti + 1}"], writes=kx)
            mu = sb["stats1"][:, 2 * ti: 2 * ti + 1]
            rs = sb["stats1"][:, 2 * ti + 1: 2 * ti + 2]
            p.op("DVE", TS(xn, xt, mu, rs, ALU.subtract, ALU.mult), reads=kx + ["stats1"], writes=kn)
            for dh in range(2):
                p.op("DVE", TT(xn[:, 512 * dh: 512 * (dh + 1)], xn[:, 512 * dh: 512 * (dh + 1)],
                               ps[f"PS{2 + dh}"], ALU.add), reads=kn + [f"PS{2 + dh}"], writes=kn)
            p.dma("ACT", [DMA(hb["xa_h"][128 * ti: 128 * (ti + 1), :], xn)],
                  reads=kn, writes=[f"xa.{ti}"])


def emit_moe(cx):
    p, hb, sb, ps = cx.p, cx.hb, cx.sb, cx.ps
    T = cx.T
    NT = T // 128
    SC = cx.SC
    PAD = pad_for(T)
    NCHS = PAD // 256
    scr = sb["SCR"]
    scrb = scr.bitcast(BF16)
    scri = scr.bitcast(I32)
    xf = sb["XF32"]

    def K(slab, lo, hi):
        return [f"{slab}.{b}" for b in range(lo // 1024, (hi + 1023) // 1024)]

    def KB(slab, lo, hi):
        return K(slab, lo // 2, hi // 2)

    c0, c1 = sb["col0"], sb["col1"]
    xt = scr[:, 0:1024]
    xn = scr[:, 1024:2048]
    sq = scr[:, 2048:3072]
    rf = scr[:, 3072:4096]
    rmtm = scrb[:, 8192:9216]
    n2w = scr[:, 4736:5760]
    gw = scr[:, 5760:5792]
    gb = scr[:, 5792:5796]
    lgg = scr[:, 5796:5800]
    ohmp = scr[:, 5800:5804]
    b1c = scr[:, 5836:5868]
    posf = scr[:, 5888:5888 + NT]
    posi = scri[:, 5952:5952 + NT]
    tids = scri[:, 6016:6016 + NT]
    kx, kn = K("SCR", 0, 1024), K("SCR", 1024, 2048)
    ksq, krf = K("SCR", 2048, 3072), K("SCR", 3072, 4096)
    krmtm = KB("SCR", 8192, 9216)
    kn2w = K("SCR", 4736, 5760)
    ksm = K("SCR", 5120, 6144)
    WS = 8 * SC - T
    ohst = xf[0:4, 0:T]
    cum = xf[0:4, WS:WS + T]
    comb = xf[0:1, 0:T]
    koh = K("XF32", 0, T)
    kcum = K("XF32", WS, WS + T)
    xgb = [scrb[:, 0:2048], scrb[:, 2048:4096]]
    kxgb = [["SCR.0"], ["SCR.1"]]
    xgfmb = [scrb[:, 4096:6144], scrb[:, 6144:8192]]
    kxgfmb = [["SCR.2"], ["SCR.3"]]
    hmt = [[scrb[:, 9216:9472], scrb[:, 9472:9728]],
           [scrb[:, 12288:12544], scrb[:, 12544:12800]]]
    khmt = [["hmA0", "hmA1"], ["hmB0", "hmB1"]]
    idxb = [scri[:, 7168:7170], scri[:, 7170:7172]]
    kidxb = [["idxA"], ["idxB"]]
    b1cb = [scr[:, 5836:5868], scr[:, 5868:5900]]
    kb1b = [["b1A"], ["b1B"]]
    xfb_ = xf.bitcast(BF16)
    b2rb = xfb_[0:1, 0:4096]
    kb2 = K("XF32", 0, 2048)
    yb0 = 16 * SC - 2048
    ybuf = xfb_[:, yb0:yb0 + 2048]
    kyb = K("XF32", yb0 // 2, yb0 // 2 + 1024)
    chunks = [(e, ch) for e in range(4) for ch in range(NCHS)]
    NCHK = len(chunks)

    def emit_weights(e, hg):
        w1k = [k for h2 in range(4 * hg, 4 * hg + 4) for k in K("WHM", 1024 * h2, 1024 * (h2 + 1))]
        w2k = [f"SL2.{h2}" for h2 in range(4 * hg, 4 * hg + 4)]
        dmas = [lambda h_, e=e, h2=h2: h_.dma_start(
                    out=sb["WHM"][:, 1024 * h2: 1024 * (h2 + 1)].rearrange("p (c w) -> p c w", w=128),
                    in_=hb["W1T"][e].rearrange("(c p) (h w) -> p h c w", p=128, w=128)[:, h2])
                for h2 in range(4 * hg, 4 * hg + 4)]
        dmas.append(DMA(sb["SL2"][:, 4096 * hg: 4096 * (hg + 1)].rearrange("p (h d) -> p h d", d=1024),
                        hb["W2T"][e].rearrange("(h p) d -> p h d", p=128)[:, 4 * hg: 4 * (hg + 1)]))
        p.dma("POOL", dmas, writes=w1k + w2k)

    def prefetch(i):
        e, ch = chunks[i]
        bi = i % 2
        base = (PAD + 1) * e + 256 * ch
        extra = ["SCR.7"] if i < 2 else []
        p.dma("SP", [DMA(idxb[bi][:, s:s + 1], hb["idx_h"][base + 128 * s: base + 128 * (s + 1), :])
                     for s in range(2)],
              reads=["idxh"], writes=kidxb[bi] + extra)
        p.dma("POOL", [lambda h, s=s, bi=bi: h.indirect_dma_start(
                           xgb[bi][:, 1024 * s: 1024 * (s + 1)], None, hb["rms2t_h"][:],
                           IndirectOffsetOnAxis(idxb[bi][:, s:s + 1], 0)) for s in range(2)],
              reads=kidxb[bi] + [f"rm2.{t}" for t in range(NT)], writes=kxgb[bi])

    def emit_trans(i):
        bi = i % 2
        pbb = ps["PB1"].bitcast(BF16)
        for r in range(2):
            for ccb in (0, 4):
                j = 2 * r + ccb // 4
                half = "ab"[j // 2]
                grp = [TRANS(pbb[:, 512 * j + 128 * (cc - ccb): 512 * j + 128 * (cc - ccb + 1)],
                             xgb[bi][:, 1024 * r + 128 * cc: 1024 * r + 128 * (cc + 1)], sb["identbf"])
                       for cc in range(ccb, ccb + 4)]
                p.op("PE", grp, reads=kxgb[bi] + ["identbf"], writes=[f"PB1.{half}"])
                dst3 = xgfmb[bi].rearrange("p (c t) -> p c t", t=256)[:, ccb:ccb + 4, 128 * r: 128 * (r + 1)]
                p.op("DVE", CP(dst3, pbb[:, 512 * j: 512 * j + 512]
                               .rearrange("p (c t) -> p c t", t=128)),
                     reads=[f"PB1.{half}"], writes=kxgfmb[bi])

    for hg in range(8):
        emit_weights(0, hg)
    p.dma("SP", [DMA(b1cb[0], hb["b1cols"][0])], writes=kb1b[0] + ["SCR.5"])
    p.dma("SP", [DMA(n2w, hb["n2wB"][:]),
                 DMA(gw.rearrange("p (c e) -> p c e", c=8), hb["gwT"].rearrange("(c p) e -> p c e", p=128)),
                 DMA(gb, hb["gbB"][0:128, 0:4]),
                 DMA(sb["eoffc"][0:4, 0:1], hb["EOFF"][:])], writes=kn2w + ksm + ["eoffc"])
    for ti in range(NT):
        p.dma("SP", [DMA(xt, hb["xa_h"][128 * ti: 128 * (ti + 1), :])],
              reads=[f"xa.{ti}"], writes=kx)
        p.op("ACT", ACTV(sq, xt, AF.Square, accum_out=c1), reads=kx, writes=ksq + ["col1"])
        p.op("DVE", TS(c1, c1, 1.0 / 1024.0, 1e-6, ALU.mult, ALU.add), reads=["col1"], writes=["col1"])
        p.op("ACT", ACTV(c1, c1, AF.Ln), reads=["col1"], writes=["col1"])
        p.op("ACT", ACTV(c0, c1, AF.Exp, scale=-0.5), reads=["col1"], writes=["col0"])
        p.op("DVE", TS1(xn, xt, c0, ALU.mult), reads=kx + ["col0"], writes=kn)
        p.op("DVE", TT(xn, xn, n2w, ALU.mult), reads=kn + kn2w, writes=kn)
        p.op("DVE", CP(rmtm, xn), reads=kn, writes=krmtm)
        p.dma("ACT", [DMA(hb["rms2t_h"][128 * ti: 128 * (ti + 1), :], rmtm)],
              reads=krmtm, writes=[f"rm2.{ti}"])
        for ccb in (0, 4):
            half = "ab"[ccb // 4]
            grp = [TRANS(ps["PB0"][:, 512 * (ccb // 4) + 128 * (cc - ccb):
                                   512 * (ccb // 4) + 128 * (cc - ccb + 1)],
                         xn[:, 128 * cc: 128 * (cc + 1)], sb["ident"])
                   for cc in range(ccb, ccb + 4)]
            p.op("PE", grp, reads=kn + ["ident"], writes=[f"PB0.{half}"])
            p.op("DVE", CP(rf[:, 128 * ccb: 128 * (ccb + 4)],
                           ps["PB0"][:, 512 * (ccb // 4): 512 * (ccb // 4) + 512]),
                 reads=[f"PB0.{half}"], writes=krf)
        grp = [MM(ps["PS0"][:, 0:4], rf[:, 128 * mc: 128 * (mc + 1)],
                  gw[:, 4 * mc: 4 * (mc + 1)], mc == 0, mc == 7) for mc in range(8)]
        p.op("PE", grp, reads=krf + ksm, writes=["PS0"])
        p.op("DVE", TT(lgg, ps["PS0"][:, 0:4], gb, ALU.add), reads=["PS0"] + ksm, writes=ksm)
        p.op("DVE", RMAX(c0, lgg), reads=ksm, writes=["col0"])
        p.op("DVE", TS1(ohmp, lgg, c0, ALU.is_ge), reads=ksm + ["col0"], writes=ksm)
        p.op("PE", TRANS(ps["PS1"][0:4, 0:128], ohmp, sb["ident"]), reads=ksm + ["ident"], writes=["PS1"])
        p.op("DVE", CP(xf[0:4, 128 * ti: 128 * (ti + 1)], ps["PS1"][0:4, 0:128]),
             reads=["PS1"], writes=K("XF32", 128 * ti, 128 * (ti + 1)))
    p.dma("SP", [DMA(tids, hb["TIDS"][:])], writes=ksm)
    p.dma("POOL", [DMA(hb["idx_h"][:], hb["BIGIDX"][:])], writes=["idxh"])
    p.op("DVE", lambda h, o=cum, a=ohst: h.tensor_tensor_scan(o, a, a, 0.0, ALU.add, ALU.max),
         reads=koh, writes=kcum)
    p.op("DVE", TS(cum, cum, 1.0, float(PAD), ALU.subtract, ALU.min), reads=kcum, writes=kcum)
    p.op("DVE", TS1(cum, cum, sb["eoffc"][0:4, 0:1], ALU.add), reads=kcum + ["eoffc"], writes=kcum)
    p.op("DVE", TT(cum, cum, ohst, ALU.mult), reads=kcum + koh, writes=kcum)
    for s in range(T // 512):
        pm = "PS0" if s % 2 == 0 else "PS1"
        p.op("PE", MM(ps[pm][0:1, 0:512], sb["onesf"][0:4, 0:1],
                      cum[0:4, 512 * s: 512 * (s + 1)], True, True),
             reads=kcum + ["onesf"], writes=[pm])
        p.op("DVE", CP(comb[0:1, 512 * s: 512 * (s + 1)], ps[pm][0:1, 0:512]),
             reads=[pm], writes=K("XF32", 512 * s, 512 * (s + 1)))
    for tb in range((NT + 7) // 8):
        n_in = min(8, NT - 8 * tb)
        half = "ab"[tb % 2]
        grp = [TRANS(ps["PB0"][:, 512 * (tb % 2) + j: 512 * (tb % 2) + j + 1],
                     comb[0:1, 128 * (8 * tb + j): 128 * (8 * tb + j + 1)],
                     sb["ident"][0:1, 0:1]) for j in range(n_in)]
        p.op("PE", grp, reads=koh + ["ident"], writes=[f"PB0.{half}"])
        p.op("DVE", CP(posf[:, 8 * tb: 8 * tb + n_in],
                       ps["PB0"][:, 512 * (tb % 2): 512 * (tb % 2) + n_in]),
             reads=[f"PB0.{half}"], writes=ksm)
    p.op("DVE", CP(posi, posf), reads=ksm, writes=ksm)
    for tt_ in range(NT):
        p.dma("POOL", [lambda h, tt_=tt_: h.indirect_dma_start(
                           hb["idx_h"][:], IndirectOffsetOnAxis(posi[:, tt_:tt_ + 1], 0),
                           tids[:, tt_:tt_ + 1], None)],
              reads=ksm, writes=["idxh"])
    p.dma("POOL", [DMA(b2rb, hb["B2ROWS"][:])], writes=kb2)
    p.op("DVE", MEMSET(xgb[0], 0.0), writes=kxgb[0])
    p.op("DVE", MEMSET(xgb[1], 0.0), writes=kxgb[1])
    prefetch(0)
    emit_trans(0)
    for i, (e, ch) in enumerate(chunks):
        bi = i % 2
        xgfm = xgfmb[bi]
        kxgfm = kxgfmb[bi]
        hm01 = hmt[bi]
        khm01 = khmt[bi]
        b1c_ = b1cb[e % 2]
        kb1_ = kb1b[e % 2]
        if i + 1 < NCHK:
            prefetch(i + 1)
        nexte = e + 1 if (ch == NCHS - 1 and e < 3) else None

        def moe_mm1(ht):
            half = ht % 2
            pb = ps["PB0"][:, 512 * half: 512 * half + 256]
            w1keys = K("WHM", 1024 * ht, 1024 * (ht + 1))
            grp = [MM(pb, sb["WHM"][:, 1024 * ht + 128 * mc: 1024 * ht + 128 * (mc + 1)],
                      xgfm[:, 256 * mc: 256 * (mc + 1)], mc == 0, mc == 7) for mc in range(8)]
            p.op("PE", grp, reads=w1keys + kxgfm, writes=[f"PB0.{'ab'[half]}"])

        def moe_evac(ht):
            half = ht % 2
            pb = ps["PB0"][:, 512 * half: 512 * half + 256]
            hm_ = hm01[half]
            kh_ = [khm01[half]]
            if i < 2:
                kh_ = kh_ + (["SCR.4"] if bi == 0 else ["SCR.6"])
            bias_c = b1c_[:, ht:ht + 1]
            if half == 0:
                p.op("DVE", TS(hm_, pb, bias_c, 0.0, ALU.add, ALU.max),
                     reads=[f"PB0.{'ab'[half]}"] + kb1_, writes=kh_)
            else:
                p.op("ACT", ACTV(hm_, pb, AF.Relu, bias=bias_c),
                     reads=[f"PB0.{'ab'[half]}"] + kb1_, writes=kh_)

        def moe_mm2(ht):
            half = ht % 2
            hm_ = hm01[half]
            kh_ = [khm01[half]]
            for tt_ in range(2):
                for dh in range(2):
                    pm = f"PS{2 * tt_ + dh}"
                    p.op("PE", MM(ps[pm], hm_[:, 128 * tt_: 128 * (tt_ + 1)],
                                  sb["SL2"][:, 1024 * ht + 512 * dh: 1024 * ht + 512 * (dh + 1)],
                                  ht == 0, False),
                         reads=kh_ + [f"SL2.{ht}"], writes=[pm])

        moe_mm1(0)
        moe_evac(0)
        for ht in range(32):
            if ht + 1 < 32:
                moe_mm1(ht + 1)
                moe_evac(ht + 1)
            moe_mm2(ht)
            if nexte is not None and ht % 4 == 3:
                emit_weights(nexte, ht // 4)
                if ht == 3:
                    p.dma("SP", [DMA(b1cb[nexte % 2], hb["b1cols"][nexte])],
                          writes=kb1b[nexte % 2] + (["SCR.5"] if nexte == 1 else []))
        if i + 1 < NCHK:
            emit_trans(i + 1)
        for tt_ in range(2):
            for dh in range(2):
                pm = f"PS{2 * tt_ + dh}"
                p.op("PE", MM(ps[pm], sb["onesrbf"],
                              b2rb[0:1, 1024 * e + 512 * dh: 1024 * e + 512 * (dh + 1)],
                              False, True), reads=["onesrbf"] + kb2, writes=[pm])
                p.op("DVE", CP(ybuf[:, 1024 * tt_ + 512 * dh: 1024 * tt_ + 512 * (dh + 1)], ps[pm]),
                     reads=[pm], writes=kyb)
        for t2 in range(2):
            p.dma("POOL", [lambda h, t2=t2, bi=bi: h.indirect_dma_start(
                               hb["y_h"][:], IndirectOffsetOnAxis(idxb[bi][:, t2:t2 + 1], 0),
                               ybuf[:, 1024 * t2: 1024 * (t2 + 1)], None)],
                  reads=kyb + kidxb[bi], writes=["yall"])


def emit_final(cx):
    p, hb, sb, ps = cx.p, cx.hb, cx.sb, cx.ps
    T = cx.T
    NT = T // 128
    scr = sb["SCR"]
    scrb = scr.bitcast(BF16)

    def K(slab, lo, hi):
        return [f"{slab}.{b}" for b in range(lo // 1024, (hi + 1023) // 1024)]

    xaP = [scr[:, 0:1024], scr[:, 4096:5120]]
    x1P = [scr[:, 1024:2048], scr[:, 5120:6144]]
    s1P = [scr[:, 2048:3072], scr[:, 6144:7168]]
    ybP = [scrb[:, 6144:7168], scrb[:, 14336:15360]]
    kxaP = [K("SCR", 0, 1024), K("SCR", 4096, 5120)]
    kx1P = [K("SCR", 1024, 2048), K("SCR", 5120, 6144)]
    ks1P = [K("SCR", 2048, 3072), K("SCR", 6144, 7168)]
    kybP = [K("SCR", 3072, 4096), K("SCR", 7168, 8192)]
    for ti in range(NT):
        pr = ti % 2
        xa, x1, s1, ytb = xaP[pr], x1P[pr], s1P[pr], ybP[pr]
        kxa, kx1, ks1, kyb_ = kxaP[pr], kx1P[pr], ks1P[pr], kybP[pr]
        p.dma("SP", [DMA(x1, hb["x1_h"][128 * ti: 128 * (ti + 1), :]),
                     DMA(s1, hb["s1_h"][128 * ti: 128 * (ti + 1), :])],
              reads=[f"x1.{2 * ti}", f"x1.{2 * ti + 1}", f"s1.{ti}"], writes=kx1 + ks1)
        p.dma("POOL", [DMA(xa, hb["xa_h"][128 * ti: 128 * (ti + 1), :]),
                       DMA(ytb, hb["y_h"][128 * ti: 128 * (ti + 1), :])],
              reads=["yall", f"xa.{ti}"], writes=kxa + kyb_)
        p.op("DVE", TT(xa, ytb, xa, ALU.add), reads=kyb_ + kxa, writes=kxa)
        p.op("ACT", ACTV(xa, xa, AF.Silu), reads=kxa, writes=kxa)
        p.op("DVE", TT(xa, xa, s1, ALU.mult), reads=kxa + ks1, writes=kxa)
        p.op("DVE", TT(xa, xa, x1, ALU.add), reads=kxa + kx1, writes=kxa)
        p.dma("ACT", [DMA(hb["out"][128 * ti: 128 * (ti + 1), :], xa)],
              reads=kxa, writes=[f"out.{ti}"])


def build_program(T, parts=("l0", "l1", "moe", "final"), scratch_kinds=None, l0_stop=None, l1_stop=None, SC=None):
    nc = bass.Bass("TRN2", target_bir_lowering=False)
    cx = Ctx(nc, T, SC)
    declare_io(cx)
    sk = dict(scratch_kinds or {})
    for n in ("x1_h", "s1_h", "xa_h"):
        cx.dram(n, (T, 1024), F32, sk.get(n, "Internal"))
    cx.dram("y_h", (T, 1024), BF16, sk.get("y_h", "Internal"))
    for n in ("qr_h",):
        cx.dram(n, (1024, T), BF16, sk.get(n, "Internal"))
    cx.dram("rms2t_h", (T, 1024), BF16, sk.get("rms2t_h", "Internal"))
    cx.dram("idx_h", (4 * (pad_for(T) + 1), 1), I32, sk.get("idx_h", "Internal"))
    if "final" in parts:
        cx.dram("out", (T, 1024), F32, "ExternalOutput")
    alloc_globals(cx)
    alloc_l0(cx)
    alloc_l1(cx)
    emit_load_consts(cx)
    if "l0" in parts:
        emit_l0(cx, stop=l0_stop)
    if "l1" in parts:
        emit_l1(cx, stop=l1_stop)
    if "moe" in parts:
        emit_moe(cx)
    if "final" in parts:
        emit_final(cx)
    cx.p.wait_all_done()
    cx.p.finalize()
    return nc, cx


def make_in_map(x_slice, host, T):
    m = {"x": np.ascontiguousarray(x_slice)}
    m.update(host)
    return m


def alloc_l1(cx):
    T = cx.T
    NT = T // 128
    cx.sbuf("stats1", (128, 2 * NT), F32)
    cx.sbuf("gbBs", (128, 4), F32)
    cx.sbuf("hdmi", (128, 1), I32)
    cx.sbuf("hdmf", (128, 1), F32)


def emit_l1(cx, stop=None):
    p, hb, sb, ps = cx.p, cx.hb, cx.sb, cx.ps
    T = cx.T
    SC = cx.SC
    NT = T // 128
    NSUP = NT // 4
    sl2 = sb["SL2"]
    scr = sb["SCR"]
    scrb = scr.bitcast(BF16)
    xf = sb["XF32"]
    xfb = xf.bitcast(BF16)

    def K(slab, lo, hi):
        return [f"{slab}.{b}" for b in range(lo // 1024, (hi + 1023) // 1024)]

    def KB(slab, lo, hi):
        return K(slab, lo // 2, hi // 2)

    def SK(lo, hi):
        return [f"SL2.{b}" for b in range(lo // 1024, (hi + 1023) // 1024)]

    c0, c1, c2, c3 = sb["col0"], sb["col1"], sb["col2"], sb["col3"]

    xt = scr[:, 0:1024]
    xn = scr[:, 1024:2048]
    sq = scr[:, 2048:3072]
    kx, kn, ksq = K("SCR", 0, 1024), K("SCR", 1024, 2048), K("SCR", 2048, 3072)
    whmK = lambda cc: K("WHM", T * cc, T * (cc + 1))
    for ti in range(NT):
        p.dma("SP", [DMA(xt, hb["x1_h"][128 * ti: 128 * (ti + 1), :])],
              reads=[f"x1.{2 * ti}", f"x1.{2 * ti + 1}"], writes=kx)
        mu = sb["stats1"][:, 2 * ti: 2 * ti + 1]
        rs = sb["stats1"][:, 2 * ti + 1: 2 * ti + 2]
        p.op("DVE", RSUM(c0, xt), reads=kx, writes=["col0"])
        p.op("ACT", ACTV(sq, xt, AF.Square, accum_out=c1), reads=kx, writes=ksq + ["col1"])
        p.op("DVE", TS1(mu, c0, 1.0 / 1024.0, ALU.mult), reads=["col0"], writes=["stats1"])
        p.op("DVE", TT(c0, mu, mu, ALU.mult), reads=["stats1"], writes=["col0"])
        p.op("DVE", TS(c1, c1, 1.0 / 1024.0, 1e-5, ALU.mult, ALU.add), reads=["col1"], writes=["col1"])
        p.op("DVE", TT(c1, c1, c0, ALU.subtract), reads=["col0", "col1"], writes=["col1"])
        p.op("ACT", ACTV(c1, c1, AF.Ln), reads=["col1"], writes=["col1"])
        p.op("ACT", ACTV(rs, c1, AF.Exp, scale=-0.5), reads=["col1"], writes=["stats1"])
        p.op("DVE", TS(xn, xt, mu, rs, ALU.subtract, ALU.mult), reads=kx + ["stats1"], writes=kn)
        for ccb in (0, 4):
            grp = [TRANS(ps["PB0"][:, 128 * (cc - ccb): 128 * (cc - ccb + 1)],
                         xn[:, 128 * cc: 128 * (cc + 1)], sb["ident"])
                   for cc in range(ccb, ccb + 4)]
            p.op("PE", grp, reads=kn + ["ident"], writes=["PB0.a"])
            for cc in range(ccb, ccb + 4):
                srcp = ps["PB0"][:, 128 * (cc - ccb): 128 * (cc - ccb + 1)]
                p.op("ACT", ACP(sb["WHM"][:, T * cc + 128 * ti: T * cc + 128 * (ti + 1)], srcp),
                     reads=["PB0.a"], writes=whmK(cc))
    xn1all = [k for cc in range(8) for k in whmK(cc)]
    if stop == "ln1":
        return

    q1T = sl2[:, 0:T]
    k1T = sl2[:, T:2 * T]
    v1a = sl2[:, 2 * T:2 * T + 132 * NT]
    base = 2 * T + 132 * NT
    womb = sl2[:, base:base + 1024]
    wqb = sl2[:, base + 1024:base + 2048]
    wkb = sl2[:, base + 2048:base + 3072]
    wvb = sl2[:, base + 3072:base + 4096]
    abf = sl2[:, base + 4096:base + 4608]
    wo1f = sl2.bitcast(F32)[:, (base + 4608) // 2: (base + 4608) // 2 + 1024]
    kq1, kk1 = SK(0, T), SK(T, 2 * T)
    kv1 = SK(2 * T, 2 * T + 132 * NT)
    kwom, kwq, kwk, kwv = (SK(base, base + 1024), SK(base + 1024, base + 2048),
                           SK(base + 2048, base + 3072), SK(base + 3072, base + 4096))
    kabf = SK(base + 4096, base + 4608)
    kwo1 = SK(base + 4608, base + 6656)
    p.dma("POOL", [
        DMA(wqb.rearrange("p (c d) -> p c d", c=8), hb["w_q1"].rearrange("(c p) d -> p c d", p=128)),
        DMA(wkb.rearrange("p (c d) -> p c d", c=8), hb["w_k1"].rearrange("(c p) d -> p c d", p=128)),
        DMA(wvb.rearrange("p (c d) -> p c d", c=8), hb["w_v1"].rearrange("(c p) d -> p c d", p=128)),
    ], writes=kwq + kwk + kwv)
    p.dma("SP", [DMA(wo1f, hb["w_o1"][:]), DMA(sb["hdmi"], hb["hdm"][:])],
          writes=kwo1 + ["hdmi"])
    p.op("DVE", CP(sb["hdmf"], sb["hdmi"]), reads=["hdmi"], writes=["hdmf"])
    p.op("DVE", TS1(womb, wo1f, sb["hdmf"], ALU.mult), reads=kwo1 + ["hdmf"], writes=kwom)
    for tch in range(T // 512):
        for wj, (which, wb, dst, kdst) in enumerate((("q", wqb, q1T, kq1), ("k", wkb, k1T, kk1))):
            pk = "PS0" if (2 * tch + wj) % 2 == 0 else "PS2"
            grp = [MM(ps[pk], wb[:, 128 * mc: 128 * (mc + 1)],
                      sb["WHM"][:, T * mc + 512 * tch: T * mc + 512 * (tch + 1)],
                      mc == 0, mc == 7) for mc in range(8)]
            p.op("PE", grp, reads=xn1all + (kwq if which == "q" else kwk), writes=[pk])
            p.op("DVE", CP(dst[:, 512 * tch: 512 * (tch + 1)], ps[pk]),
                 reads=[pk], writes=kdst)
    for ti in range(NT):
        pk = "PS1" if ti % 2 == 0 else "PS3"
        grp = [MM(ps[pk][:, 0:128], sb["WHM"][:, T * mc + 128 * ti: T * mc + 128 * (ti + 1)],
                  wvb[:, 128 * mc: 128 * (mc + 1)], mc == 0, mc == 7) for mc in range(8)]
        p.op("PE", grp, reads=xn1all + kwv, writes=[pk])
        p.op("DVE", CP(v1a[:, 132 * ti: 132 * ti + 128], ps[pk][:, 0:128]),
             reads=[pk], writes=kv1)
    p.op("DVE", MEMSET(v1a.rearrange("p (t c) -> p t c", c=132)[:, :, 128:129], 1.0),
         reads=[], writes=kv1)
    abfb = sl2[:, base + 4608 + 2048:base + 4608 + 2048 + 512]
    kabfb = SK(base + 6656, base + 7168)
    for isup in range(NSUP):
        def st1(jt, isup=isup):
            half = jt % 2
            p.op("PE", MM(ps["PB0"][:, 512 * half: 512 * (half + 1)],
                          k1T[:, 128 * jt: 128 * (jt + 1)],
                          q1T[:, 512 * isup: 512 * (isup + 1)], True, True),
                 reads=kq1 + kk1, writes=[f"PB0.{'ab'[half]}"])

        def exp1(jt):
            half = jt % 2
            ab = abf if half == 0 else abfb
            kab = kabf if half == 0 else kabfb
            p.op("ACT", ACTV(ab, ps["PB0"][:, 512 * half: 512 * (half + 1)], AF.Exp,
                             bias=sb["c_m4"], scale=float(1.0 / np.sqrt(128.0))),
                 reads=[f"PB0.{'ab'[half]}", "c_m4"], writes=kab)

        def num1(jt):
            half = jt % 2
            ab = abf if half == 0 else abfb
            kab = kabf if half == 0 else kabfb
            for s in range(4):
                p.op("PE", MM(ps[f"PS{s}"][:, 0:129], ab[:, 128 * s: 128 * (s + 1)],
                              v1a[:, 132 * jt: 132 * jt + 129], jt == 0, jt == NT - 1),
                     reads=kab + kv1, writes=[f"PS{s}"])

        st1(0)
        exp1(0)
        for jt in range(NT):
            if jt + 1 < NT:
                st1(jt + 1)
                exp1(jt + 1)
            num1(jt)
        for s in range(4):
            ti = 4 * isup + s
            p.op("DVE", RECIP(c0, ps[f"PS{s}"][:, 128:129]), reads=[f"PS{s}"], writes=["col0"])
            ot = scrb[:, 0:128]
            p.op("DVE", TS1(ot, ps[f"PS{s}"][:, 0:128], c0, ALU.mult),
                 reads=[f"PS{s}", "col0"], writes=K("SCR", 0, 1024))
            p.op("PE", TRANS(ps["PB0"].bitcast(BF16)[:, 0:128], ot, sb["identbf"]),
                 reads=K("SCR", 0, 1024) + ["identbf"], writes=["PB0.a"])
            otT = scrb[:, 128:256]
            p.op("DVE", CP(otT, ps["PB0"].bitcast(BF16)[:, 0:128]),
                 reads=["PB0.a"], writes=K("SCR", 0, 1024))
            for dh in range(2):
                p.op("PE", MM(ps["PB0"][:, 0:512], otT, womb[:, 512 * dh: 512 * (dh + 1)], True, True),
                     reads=K("SCR", 0, 1024) + kwom, writes=["PB0.a"])
                yt = scr[:, 1024:1536]
                p.op("DVE", CP(yt, ps["PB0"][:, 0:512]), reads=["PB0.a"], writes=K("SCR", 1024, 2048))
                p.dma("ACT", [DMA(hb["s1_h"][128 * ti: 128 * (ti + 1), 512 * dh: 512 * (dh + 1)], yt)],
                      reads=K("SCR", 1024, 2048), writes=[f"s1.{ti}"])

    if stop == "attn":
        return
    wqmb = sl2[:, 0:8192]
    wdkvb = sl2[:, 8192:12288]
    kwqm, kwdkv = SK(0, 8192), SK(8192, 12288)
    p.dma("POOL", [
        DMA(wqmb.rearrange("p (c d) -> p c d", c=8), hb["wqmT"].rearrange("(c p) d -> p c d", p=128)),
        DMA(wdkvb.rearrange("p (c d) -> p c d", c=8), hb["wdkvT"].rearrange("(c p) d -> p c d", p=128)),
    ], writes=kwqm + kwdkv)
    n1w = sl2.bitcast(F32)[:, 10240:11264]
    kn1w = SK(20480, 22528)
    p.dma("SP", [DMA(n1w, hb["n1wB"][:])], writes=kn1w)
    rm = scr[:, 1024:2048]
    rmb = scrb[:, 4096:5120]
    rmf = scrb[:, 5120:6144]
    qtm = scr[:, 3072:4096]
    tA = scr[:, 4096:5120]
    tB = scr[:, 5120:6144]
    krm, krmb, krmf = K("SCR", 1024, 2048), K("SCR", 2048, 2560), K("SCR", 2560, 3072)
    kqtm, kA, kB = K("SCR", 3072, 4096), K("SCR", 4096, 5120), K("SCR", 5120, 6144)
    latfm = xfb
    klat = lambda lc: KB("XF32", lc * T, (lc + 1) * T)
    latall = [k for lc in range(4) for k in klat(lc)]
    for ti in range(NT):
        p.dma("SP", [DMA(xt, hb["x1_h"][128 * ti: 128 * (ti + 1), :])],
              reads=[f"x1.{2 * ti}", f"x1.{2 * ti + 1}"], writes=kx)
        mu = sb["stats1"][:, 2 * ti: 2 * ti + 1]
        rs = sb["stats1"][:, 2 * ti + 1: 2 * ti + 2]
        p.op("DVE", TS(xn, xt, mu, rs, ALU.subtract, ALU.mult), reads=kx + ["stats1"], writes=kn)
        p.op("ACT", ACTV(sq, xn, AF.Square, accum_out=c1), reads=kn, writes=ksq + ["col1"])
        p.op("DVE", TS(c1, c1, 1.0 / 1024.0, 1e-6, ALU.mult, ALU.add), reads=["col1"], writes=["col1"])
        p.op("ACT", ACTV(c1, c1, AF.Ln), reads=["col1"], writes=["col1"])
        p.op("ACT", ACTV(c0, c1, AF.Exp, scale=-0.5), reads=["col1"], writes=["col0"])
        p.op("DVE", TS1(rm, xn, c0, ALU.mult), reads=kn + ["col0"], writes=krm)
        p.op("DVE", TT(rm, rm, n1w, ALU.mult), reads=krm + kn1w, writes=krm)
        p.op("DVE", CP(rmb, rm), reads=krm, writes=krmb)
        for ccb in (0, 4):
            grp = [TRANS(ps["PB0"].bitcast(BF16)[:, 128 * (cc - ccb): 128 * (cc - ccb + 1)],
                         rmb[:, 128 * cc: 128 * (cc + 1)], sb["identbf"])
                   for cc in range(ccb, ccb + 4)]
            p.op("PE", grp, reads=krmb + ["identbf"], writes=["PB0.a"])
            p.op("ACT", ACP(rmf[:, 128 * ccb: 128 * (ccb + 4)],
                            ps["PB0"].bitcast(BF16)[:, 0:512]),
                 reads=["PB0.a"], writes=krmf)
        for dh in range(2):
            grp = [MM(ps[f"PS{dh}"], rmf[:, 128 * mc: 128 * (mc + 1)],
                      wqmb[:, 1024 * mc + 512 * dh: 1024 * mc + 512 * (dh + 1)],
                      mc == 0, mc == 7) for mc in range(8)]
            p.op("PE", grp, reads=krmf + kwqm, writes=[f"PS{dh}"])
            p.op("ACT", ACP(qtm[:, 512 * dh: 512 * (dh + 1)], ps[f"PS{dh}"]),
                 reads=[f"PS{dh}"], writes=kqtm)
        nslot = min(NT, SC // 128)
        if ti % nslot == 0:
            p.dma("SP", [DMA(sb["cost"][:, 0:32 * nslot], hb["COSt"][:, 32 * ti: 32 * (ti + nslot)]),
                         DMA(sb["sint"][:, 0:32 * nslot], hb["SINt"][:, 32 * ti: 32 * (ti + nslot)])],
                  writes=["cost", "sint"])
        cosv = sb["cost"][:, 32 * (ti % nslot): 32 * (ti % nslot) + 32].unsqueeze(1).broadcast_to([128, 16, 32])
        sinv = sb["sint"][:, 32 * (ti % nslot): 32 * (ti % nslot) + 32].unsqueeze(1).broadcast_to([128, 16, 32])
        s3 = qtm.rearrange("p (h d) -> p h d", d=64)
        a3 = tA.rearrange("p (h d) -> p h d", d=64)
        b3 = tB.rearrange("p (h d) -> p h d", d=64)
        p.op("DVE", TT(a3[:, :, 0:32], s3[:, :, 0:32], cosv, ALU.mult), reads=kqtm + ["cost"], writes=kA)
        p.op("DVE", TT(a3[:, :, 32:64], s3[:, :, 32:64], sinv, ALU.mult), reads=kqtm + ["sint"], writes=kA)
        p.op("DVE", TT(b3[:, :, 0:32], s3[:, :, 32:64], cosv, ALU.mult), reads=kqtm + ["cost"], writes=kB)
        p.op("DVE", TT(b3[:, :, 32:64], s3[:, :, 0:32], sinv, ALU.mult), reads=kqtm + ["sint"], writes=kB)
        p.op("DVE", TT(s3[:, :, 0:32], a3[:, :, 0:32], a3[:, :, 32:64], ALU.subtract), reads=kA, writes=kqtm)
        p.op("DVE", TT(s3[:, :, 32:64], b3[:, :, 0:32], b3[:, :, 32:64], ALU.add), reads=kB, writes=kqtm)
        qbf = scrb[:, 4096 + 2048:4096 + 3072]
        qbf = scrb[:, 12288:13312]
        kqbf = K("SCR", 6144, 6656 + 0) if False else K("SCR", 6144, 6656)
        kqbf = KB("SCR", 12288, 13312)
        p.op("DVE", CP(qbf, qtm), reads=kqtm, writes=kqbf)
        for ccb in (0, 4):
            grp = [TRANS(ps["PB0"].bitcast(BF16)[:, 128 * (cc - ccb): 128 * (cc - ccb + 1)],
                         qbf[:, 128 * cc: 128 * (cc + 1)], sb["identbf"])
                   for cc in range(ccb, ccb + 4)]
            p.op("PE", grp, reads=kqbf + ["identbf"], writes=["PB0.a"])
            p.op("ACT", ACP(scrb[:, 13312 + 128 * ccb: 13312 + 128 * (ccb + 4)],
                            ps["PB0"].bitcast(BF16)[:, 0:512]),
                 reads=["PB0.a"], writes=KB("SCR", 13312, 14336))
        p.dma("ACT", [DMA(hb["qr_h"].rearrange("(c p) t -> p c t", p=128)[:, :, 128 * ti: 128 * (ti + 1)],
                          scrb[:, 13312:14336].rearrange("p (c t) -> p c t", c=8))],
              reads=KB("SCR", 13312, 14336), writes=[f"qr.{ti}"])
        grp = [MM(ps["PS2"], rmf[:, 128 * mc: 128 * (mc + 1)],
                  wdkvb[:, 512 * mc: 512 * (mc + 1)], mc == 0, mc == 7) for mc in range(8)]
        p.op("PE", grp, reads=krmf + kwdkv, writes=["PS2"])
        lat = scr[:, 3072:3584]
        klatt = K("SCR", 3072, 4096)
        p.op("DVE", RSUM(c0, ps["PS2"]), reads=["PS2"], writes=["col0"])
        p.op("ACT", ACTV(lat, ps["PS2"], AF.Square, accum_out=c1), reads=["PS2"], writes=klatt + ["col1"])
        p.op("DVE", TS1(c0, c0, 1.0 / 512.0, ALU.mult), reads=["col0"], writes=["col0"])
        p.op("DVE", TT(c2, c0, c0, ALU.mult), reads=["col0"], writes=["col2"])
        p.op("DVE", TS(c1, c1, 1.0 / 512.0, 1e-5, ALU.mult, ALU.add), reads=["col1"], writes=["col1"])
        p.op("DVE", TT(c1, c1, c2, ALU.subtract), reads=["col1", "col2"], writes=["col1"])
        p.op("ACT", ACTV(c1, c1, AF.Ln), reads=["col1"], writes=["col1"])
        p.op("ACT", ACTV(c2, c1, AF.Exp, scale=-0.5), reads=["col1"], writes=["col2"])
        p.op("DVE", TS(lat, ps["PS2"], c0, c2, ALU.subtract, ALU.mult),
             reads=["PS2", "col0", "col2"], writes=klatt)
        lngv = sl2.bitcast(F32)[:, 11264:11776]
        lnbv = sl2.bitcast(F32)[:, 11776:12288]
        klng, klnb = SK(22528, 23552), SK(23552, 24576)
        if ti == 0:
            p.dma("SP", [DMA(lngv, hb["lngB"][:]), DMA(lnbv, hb["lnbB"][:])], writes=klng + klnb)
        p.op("DVE", TT(lat, lat, lngv, ALU.mult), reads=klatt + klng, writes=klatt)
        p.op("DVE", TT(lat, lat, lnbv, ALU.add), reads=klatt + klnb, writes=klatt)
        latb = scrb[:, 7168:7680]
        klatb = K("SCR", 3584, 4096)
        p.op("DVE", CP(latb, lat), reads=klatt, writes=klatb)
        grp = [TRANS(ps["PB0"].bitcast(BF16)[:, 128 * lc: 128 * (lc + 1)],
                     latb[:, 128 * lc: 128 * (lc + 1)], sb["identbf"]) for lc in range(4)]
        p.op("PE", grp, reads=klatb + ["identbf"], writes=["PB0.a"])
        for lc in range(4):
            p.op("ACT", ACP(latfm[:, T * lc + 128 * ti: T * lc + 128 * (ti + 1)],
                            ps["PB0"].bitcast(BF16)[:, 128 * lc: 128 * (lc + 1)]),
                 reads=["PB0.a"], writes=klat(lc))

    if stop == "proj":
        return
    wukb = scrb[:, 0:4096]
    wuvb = scrb[:, 12288:16384]
    kwuk = KB("SCR", 0, 4096)
    kwuv = KB("SCR", 12288, 16384)
    p.dma("POOL", [
        DMA(wukb.rearrange("p (c d) -> p c d", c=4), hb["wukT"].rearrange("(c p) d -> p c d", p=128)),
        DMA(wuvb.rearrange("p (c d) -> p c d", c=4), hb["wuvT"].rearrange("(c p) d -> p c d", p=128)),
    ], writes=kwuk + kwuv)
    v1m = sl2[:, 0:1040 * NT]
    kv1m = lambda jt: SK(1040 * jt, 1040 * (jt + 1))
    ktm = scr[:, 3072:4096]
    kktm = K("SCR", 3072, 4096)
    for ti in range(NT):
        for dh in range(2):
            grp = [MM(ps[f"PS{dh}"], latfm[:, T * lc + 128 * ti: T * lc + 128 * (ti + 1)],
                      wukb[:, 1024 * lc + 512 * dh: 1024 * lc + 512 * (dh + 1)],
                      lc == 0, lc == 3) for lc in range(4)]
            p.op("PE", grp, reads=latall + kwuk, writes=[f"PS{dh}"])
            p.op("ACT", ACP(ktm[:, 512 * dh: 512 * (dh + 1)], ps[f"PS{dh}"]),
                 reads=[f"PS{dh}"], writes=kktm)
        nslot = min(NT, SC // 128)
        if ti % nslot == 0:
            p.dma("SP", [DMA(sb["cost"][:, 0:32 * nslot], hb["COSt"][:, 32 * ti: 32 * (ti + nslot)]),
                         DMA(sb["sint"][:, 0:32 * nslot], hb["SINt"][:, 32 * ti: 32 * (ti + nslot)])],
                  writes=["cost", "sint"])
        cosv = sb["cost"][:, 32 * (ti % nslot): 32 * (ti % nslot) + 32].unsqueeze(1).broadcast_to([128, 16, 32])
        sinv = sb["sint"][:, 32 * (ti % nslot): 32 * (ti % nslot) + 32].unsqueeze(1).broadcast_to([128, 16, 32])
        s3 = ktm.rearrange("p (h d) -> p h d", d=64)
        a3 = tA.rearrange("p (h d) -> p h d", d=64)
        b3 = tB.rearrange("p (h d) -> p h d", d=64)
        p.op("DVE", TT(a3[:, :, 0:32], s3[:, :, 0:32], cosv, ALU.mult), reads=kktm + ["cost"], writes=kA)
        p.op("DVE", TT(a3[:, :, 32:64], s3[:, :, 32:64], sinv, ALU.mult), reads=kktm + ["sint"], writes=kA)
        p.op("DVE", TT(b3[:, :, 0:32], s3[:, :, 32:64], cosv, ALU.mult), reads=kktm + ["cost"], writes=kB)
        p.op("DVE", TT(b3[:, :, 32:64], s3[:, :, 0:32], sinv, ALU.mult), reads=kktm + ["sint"], writes=kB)
        p.op("DVE", TT(s3[:, :, 0:32], a3[:, :, 0:32], a3[:, :, 32:64], ALU.subtract), reads=kA, writes=kktm)
        p.op("DVE", TT(s3[:, :, 32:64], b3[:, :, 0:32], b3[:, :, 32:64], ALU.add), reads=kB, writes=kktm)
        kbf = scrb[:, 4096:5120]
        kkbf = KB("SCR", 4096, 5120)
        p.op("DVE", CP(kbf, ktm), reads=kktm, writes=kkbf)
        for ccb in (0, 4):
            grp = [TRANS(ps["PB0"].bitcast(BF16)[:, 128 * (cc - ccb): 128 * (cc - ccb + 1)],
                         kbf[:, 128 * cc: 128 * (cc + 1)], sb["identbf"])
                   for cc in range(ccb, ccb + 4)]
            p.op("PE", grp, reads=kkbf + ["identbf"], writes=["PB0.a"])
            for cc in range(ccb, ccb + 4):
                p.op("ACT", ACP(sb["WHM"][:, T * cc + 128 * ti: T * cc + 128 * (ti + 1)],
                                ps["PB0"].bitcast(BF16)[:, 128 * (cc - ccb): 128 * (cc - ccb + 1)]),
                     reads=["PB0.a"], writes=whmK(cc))
        for dh in range(2):
            grp = [MM(ps[f"PS{2 + dh}"], latfm[:, T * lc + 128 * ti: T * lc + 128 * (ti + 1)],
                      wuvb[:, 1024 * lc + 512 * dh: 1024 * lc + 512 * (dh + 1)],
                      lc == 0, lc == 3) for lc in range(4)]
            p.op("PE", grp, reads=latall + kwuv, writes=[f"PS{2 + dh}"])
            dstv = v1m[:, 1040 * ti + 520 * dh: 1040 * ti + 520 * (dh + 1)].rearrange(
                "p (h c) -> p h c", c=65)[:, :, 0:64]
            p.op("DVE", CP(dstv, ps[f"PS{2 + dh}"].rearrange("p (h c) -> p h c", c=64)),
                 reads=[f"PS{2 + dh}"], writes=kv1m(ti))
        p.op("DVE", MEMSET(v1m[:, 1040 * ti: 1040 * (ti + 1)].rearrange(
            "p (h c) -> p h c", c=65)[:, :, 64:65], 1.0), writes=kv1m(ti))

    if stop in ("kv", "kv_dump"):
        if stop == "kv_dump":
            if "rms2h" not in hb:
                cx.dram("rms2h", (1024, T), BF16, "ExternalOutput")
            p.dma("ACT", [DMA(hb["rms2h"].rearrange("(c p) t -> p c t", p=128),
                              sb["WHM"][:, 0:8 * T].rearrange("p (c t) -> p c t", c=8))],
                  reads=[k for cc in range(8) for k in whmK(cc)], writes=["rm2dump"])
        return
    qrb = scrb[:, 0:4096]
    kqrb = KB("SCR", 0, 4096)
    womtb = scrb[:, 4096:12288]
    kwomt = KB("SCR", 4096, 12288)
    attnb = scrb[:, 12288:16384]
    kattnb = KB("SCR", 12288, 16384)
    abf2 = xfb[:, 0:512]
    kabf2 = KB("XF32", 0, 1024)
    if 16 * (cx.SC) >= 12288:
        recB = xf[:, 4096:4608]
        recrow = xf[:, 4608:5120]
        krec = K("XF32", 4096, 5120)
        obf = xfb[0:64, 10240:10752]
        kobf = KB("XF32", 10240, 11264)
    else:
        b2_ = ((1040 * NT + 2047) // 2048) * 2048
        recB = sl2.bitcast(F32)[:, b2_ // 2: b2_ // 2 + 512]
        recrow = sl2.bitcast(F32)[:, b2_ // 2 + 512: b2_ // 2 + 1024]
        krec = SK(b2_, b2_ + 2048)
        obf = sl2[0:64, b2_ + 2048: b2_ + 2560]
        kobf = SK(b2_ + 2048, b2_ + 3072)
    p.dma("POOL", [DMA(womtb.rearrange("p (c d) -> p c d", c=8),
                       hb["womT"].rearrange("(c p) d -> p c d", p=128))], writes=kwomt)
    ABF = [[xfb[:, 0:512], xfb[:, 2048:2560]],
           [xfb[:, 4096:4608], xfb[:, 6144:6656]]]
    KABF = [[KB("XF32", 0, 1024), KB("XF32", 2048, 3072)],
            [KB("XF32", 4096, 5120), KB("XF32", 6144, 7168)]]
    for isup in range(NSUP):
        p.dma("SP", [DMA(qrb.rearrange("p (c t) -> p c t", c=8),
                         hb["qr_h"].rearrange("(c p) t -> p c t", p=128)[:, :, 512 * isup: 512 * (isup + 1)])],
              reads=[f"qr.{tt_}" for tt_ in range(4 * isup, 4 * isup + 4)], writes=kqrb)
        njj = 4 * isup + 4
        for qq in range(8):
            cc = qq

            STPS = [[(ps["PB0"][:, 0:512], "PB0.a"), (ps["PB0"][:, 512:1024], "PB0.b")],
                    [(ps["PS1"], "PS1"), (ps["PS3"], "PS3")]]

            def st_pair(jj, cc=cc, STPS=STPS):
                for half in range(2):
                    rb = 64 * half
                    stp, stk = STPS[jj % 2][half]
                    p.op("PE", MM(stp, sb["WHM"][rb:rb + 64, T * cc + 128 * jj: T * cc + 128 * (jj + 1)],
                                  qrb[rb:rb + 64, 512 * cc: 512 * (cc + 1)], True, True),
                         reads=whmK(cc) + kqrb, writes=[stk])

            def exp_pair(jj, isup=isup, STPS=STPS):
                dj = jj - 4 * isup
                for half in range(2):
                    ab = ABF[jj % 2][half]
                    kab = KABF[jj % 2][half]
                    stp, stk = STPS[jj % 2][half]
                    p.op("ACT", ACTV(ab, stp, AF.Exp, bias=sb["c_m4"], scale=0.125),
                         reads=[stk, "c_m4"], writes=kab)
                    if dj > 0:
                        p.op("DVE", MEMSET(ab[:, 0:128 * dj], 0.0), writes=kab)
                    if 0 <= dj <= 3:
                        p.op("DVE", TT(ab[:, 128 * dj: 128 * (dj + 1)],
                                       ab[:, 128 * dj: 128 * (dj + 1)], sb["tribf"], ALU.mult),
                             reads=kab + ["tribf"], writes=kab)

            def num_pair(jj, qq=qq, njj=njj):
                for half in range(2):
                    hh = 2 * qq + half
                    ab = ABF[jj % 2][half]
                    kab = KABF[jj % 2][half]
                    pnum = "PS0" if half == 0 else "PS2"
                    p.op("PE", MM(ps[pnum][0:65, 0:512],
                                  v1m[:, 1040 * jj + 65 * hh: 1040 * jj + 65 * (hh + 1)],
                                  ab, jj == 0, jj == njj - 1),
                         reads=kv1m(jj) + kab, writes=[pnum])

            st_pair(0)
            exp_pair(0)
            for jj in range(njj):
                if jj + 1 < njj:
                    st_pair(jj + 1)
                    exp_pair(jj + 1)
                num_pair(jj)
            for half in range(2):
                pnum = "PS0" if half == 0 else "PS2"
                p.op("DVE", RECIP(recrow[0:1, :], ps[pnum][64:65, 0:512]), reads=[pnum], writes=krec)
                p.op("PE", MM(ps["PS1"][0:64, 0:512], sb["onesrow"][:, 0:64], recrow[0:1, :], True, True),
                     reads=["onesrow"] + krec, writes=["PS1"])
                p.op("ACT", ACP(recB[0:64, :], ps["PS1"][0:64, 0:512]), reads=["PS1"], writes=krec)
                if half == 0:
                    dsta = attnb[0:64, 512 * cc: 512 * (cc + 1)]
                    p.op("DVE", TT(dsta, ps[pnum][0:64, 0:512], recB[0:64, :], ALU.mult),
                         reads=[pnum] + krec, writes=kattnb)
                else:
                    p.op("DVE", TT(obf, ps[pnum][0:64, 0:512], recB[0:64, :], ALU.mult),
                         reads=[pnum] + krec, writes=kobf)
                    p.dma("POOL", [DMA(attnb[64:128, 512 * cc: 512 * (cc + 1)], obf)],
                          reads=kobf, writes=kattnb)
        for s in range(4):
            ti = 4 * isup + s
            for dh in range(2):
                grp = [MM(ps[f"PS{2 + dh}"], attnb[:, 512 * cc2 + 128 * s: 512 * cc2 + 128 * (s + 1)],
                          womtb[:, 1024 * cc2 + 512 * dh: 1024 * cc2 + 512 * (dh + 1)],
                          cc2 == 0, cc2 == 7) for cc2 in range(8)]
                p.op("PE", grp, reads=kattnb + kwomt, writes=[f"PS{2 + dh}"])
            p.dma("SP", [DMA(xt, hb["x1_h"][128 * ti: 128 * (ti + 1), :])],
                  reads=[f"x1.{2 * ti}", f"x1.{2 * ti + 1}"], writes=kx)
            mu = sb["stats1"][:, 2 * ti: 2 * ti + 1]
            rs = sb["stats1"][:, 2 * ti + 1: 2 * ti + 2]
            p.op("DVE", TS(xn, xt, mu, rs, ALU.subtract, ALU.mult), reads=kx + ["stats1"], writes=kn)
            for dh in range(2):
                p.op("DVE", TT(xn[:, 512 * dh: 512 * (dh + 1)], xn[:, 512 * dh: 512 * (dh + 1)],
                               ps[f"PS{2 + dh}"], ALU.add), reads=kn + [f"PS{2 + dh}"], writes=kn)
            p.dma("ACT", [DMA(hb["xa_h"][128 * ti: 128 * (ti + 1), :], xn)],
                  reads=kn, writes=[f"xa.{ti}"])


def emit_moe(cx):
    p, hb, sb, ps = cx.p, cx.hb, cx.sb, cx.ps
    T = cx.T
    NT = T // 128
    SC = cx.SC
    PAD = pad_for(T)
    NCHS = PAD // 256
    scr = sb["SCR"]
    scrb = scr.bitcast(BF16)
    scri = scr.bitcast(I32)
    xf = sb["XF32"]

    def K(slab, lo, hi):
        return [f"{slab}.{b}" for b in range(lo // 1024, (hi + 1023) // 1024)]

    def KB(slab, lo, hi):
        return K(slab, lo // 2, hi // 2)

    c0, c1 = sb["col0"], sb["col1"]
    xt = scr[:, 0:1024]
    xn = scr[:, 1024:2048]
    sq = scr[:, 2048:3072]
    rf = scr[:, 3072:4096]
    rmtm = scrb[:, 8192:9216]
    n2w = scr[:, 4736:5760]
    gw = scr[:, 5760:5792]
    gb = scr[:, 5792:5796]
    lgg = scr[:, 5796:5800]
    ohmp = scr[:, 5800:5804]
    b1c = scr[:, 5836:5868]
    posf = scr[:, 5888:5888 + NT]
    posi = scri[:, 5952:5952 + NT]
    tids = scri[:, 6016:6016 + NT]
    kx, kn = K("SCR", 0, 1024), K("SCR", 1024, 2048)
    ksq, krf = K("SCR", 2048, 3072), K("SCR", 3072, 4096)
    krmtm = KB("SCR", 8192, 9216)
    kn2w = K("SCR", 4736, 5760)
    ksm = K("SCR", 5120, 6144)
    WS = 8 * SC - T
    ohst = xf[0:4, 0:T]
    cum = xf[0:4, WS:WS + T]
    comb = xf[0:1, 0:T]
    koh = K("XF32", 0, T)
    kcum = K("XF32", WS, WS + T)
    xgb = [scrb[:, 0:2048], scrb[:, 2048:4096]]
    kxgb = [["SCR.0"], ["SCR.1"]]
    xgfmb = [scrb[:, 4096:6144], scrb[:, 6144:8192]]
    kxgfmb = [["SCR.2"], ["SCR.3"]]
    hmt = [[scrb[:, 9216:9472], scrb[:, 9472:9728]],
           [scrb[:, 12288:12544], scrb[:, 12544:12800]]]
    khmt = [["hmA0", "hmA1"], ["hmB0", "hmB1"]]
    idxb = [scri[:, 7168:7170], scri[:, 7170:7172]]
    kidxb = [["idxA"], ["idxB"]]
    b1cb = [scr[:, 5836:5868], scr[:, 5868:5900]]
    kb1b = [["b1A"], ["b1B"]]
    xfb_ = xf.bitcast(BF16)
    b2rb = xfb_[0:1, 0:4096]
    kb2 = K("XF32", 0, 2048)
    yb0 = 16 * SC - 2048
    ybuf = xfb_[:, yb0:yb0 + 2048]
    kyb = K("XF32", yb0 // 2, yb0 // 2 + 1024)
    chunks = [(e, ch) for e in range(4) for ch in range(NCHS)]
    NCHK = len(chunks)

    def emit_weights(e, hg):
        w1k = [k for h2 in range(4 * hg, 4 * hg + 4) for k in K("WHM", 1024 * h2, 1024 * (h2 + 1))]
        w2k = [f"SL2.{h2}" for h2 in range(4 * hg, 4 * hg + 4)]
        dmas = [lambda h_, e=e, h2=h2: h_.dma_start(
                    out=sb["WHM"][:, 1024 * h2: 1024 * (h2 + 1)].rearrange("p (c w) -> p c w", w=128),
                    in_=hb["W1T"][e].rearrange("(c p) (h w) -> p h c w", p=128, w=128)[:, h2])
                for h2 in range(4 * hg, 4 * hg + 4)]
        dmas.append(DMA(sb["SL2"][:, 4096 * hg: 4096 * (hg + 1)].rearrange("p (h d) -> p h d", d=1024),
                        hb["W2T"][e].rearrange("(h p) d -> p h d", p=128)[:, 4 * hg: 4 * (hg + 1)]))
        p.dma("POOL", dmas, writes=w1k + w2k)

    def prefetch(i):
        e, ch = chunks[i]
        bi = i % 2
        base = (PAD + 1) * e + 256 * ch
        extra = ["SCR.7"] if i < 2 else []
        p.dma("SP", [DMA(idxb[bi][:, s:s + 1], hb["idx_h"][base + 128 * s: base + 128 * (s + 1), :])
                     for s in range(2)],
              reads=["idxh"], writes=kidxb[bi] + extra)
        p.dma("POOL", [lambda h, s=s, bi=bi: h.indirect_dma_start(
                           xgb[bi][:, 1024 * s: 1024 * (s + 1)], None, hb["rms2t_h"][:],
                           IndirectOffsetOnAxis(idxb[bi][:, s:s + 1], 0)) for s in range(2)],
              reads=kidxb[bi] + [f"rm2.{t}" for t in range(NT)], writes=kxgb[bi])

    def emit_trans(i):
        bi = i % 2
        pbb = ps["PB1"].bitcast(BF16)
        for r in range(2):
            for ccb in (0, 4):
                j = 2 * r + ccb // 4
                half = "ab"[j // 2]
                grp = [TRANS(pbb[:, 512 * j + 128 * (cc - ccb): 512 * j + 128 * (cc - ccb + 1)],
                             xgb[bi][:, 1024 * r + 128 * cc: 1024 * r + 128 * (cc + 1)], sb["identbf"])
                       for cc in range(ccb, ccb + 4)]
                p.op("PE", grp, reads=kxgb[bi] + ["identbf"], writes=[f"PB1.{half}"])
                dst3 = xgfmb[bi].rearrange("p (c t) -> p c t", t=256)[:, ccb:ccb + 4, 128 * r: 128 * (r + 1)]
                p.op("DVE", CP(dst3, pbb[:, 512 * j: 512 * j + 512]
                               .rearrange("p (c t) -> p c t", t=128)),
                     reads=[f"PB1.{half}"], writes=kxgfmb[bi])

    for hg in range(8):
        emit_weights(0, hg)
    p.dma("SP", [DMA(b1cb[0], hb["b1cols"][0])], writes=kb1b[0] + ["SCR.5"])
    p.dma("SP", [DMA(n2w, hb["n2wB"][:]),
                 DMA(gw.rearrange("p (c e) -> p c e", c=8), hb["gwT"].rearrange("(c p) e -> p c e", p=128)),
                 DMA(gb, hb["gbB"][0:128, 0:4]),
                 DMA(sb["eoffc"][0:4, 0:1], hb["EOFF"][:])], writes=kn2w + ksm + ["eoffc"])
    for ti in range(NT):
        p.dma("SP", [DMA(xt, hb["xa_h"][128 * ti: 128 * (ti + 1), :])],
              reads=[f"xa.{ti}"], writes=kx)
        p.op("ACT", ACTV(sq, xt, AF.Square, accum_out=c1), reads=kx, writes=ksq + ["col1"])
        p.op("DVE", TS(c1, c1, 1.0 / 1024.0, 1e-6, ALU.mult, ALU.add), reads=["col1"], writes=["col1"])
        p.op("ACT", ACTV(c1, c1, AF.Ln), reads=["col1"], writes=["col1"])
        p.op("ACT", ACTV(c0, c1, AF.Exp, scale=-0.5), reads=["col1"], writes=["col0"])
        p.op("DVE", TS1(xn, xt, c0, ALU.mult), reads=kx + ["col0"], writes=kn)
        p.op("DVE", TT(xn, xn, n2w, ALU.mult), reads=kn + kn2w, writes=kn)
        p.op("DVE", CP(rmtm, xn), reads=kn, writes=krmtm)
        p.dma("ACT", [DMA(hb["rms2t_h"][128 * ti: 128 * (ti + 1), :], rmtm)],
              reads=krmtm, writes=[f"rm2.{ti}"])
        for ccb in (0, 4):
            half = "ab"[ccb // 4]
            grp = [TRANS(ps["PB0"][:, 512 * (ccb // 4) + 128 * (cc - ccb):
                                   512 * (ccb // 4) + 128 * (cc - ccb + 1)],
                         xn[:, 128 * cc: 128 * (cc + 1)], sb["ident"])
                   for cc in range(ccb, ccb + 4)]
            p.op("PE", grp, reads=kn + ["ident"], writes=[f"PB0.{half}"])
            p.op("DVE", CP(rf[:, 128 * ccb: 128 * (ccb + 4)],
                           ps["PB0"][:, 512 * (ccb // 4): 512 * (ccb // 4) + 512]),
                 reads=[f"PB0.{half}"], writes=krf)
        grp = [MM(ps["PS0"][:, 0:4], rf[:, 128 * mc: 128 * (mc + 1)],
                  gw[:, 4 * mc: 4 * (mc + 1)], mc == 0, mc == 7) for mc in range(8)]
        p.op("PE", grp, reads=krf + ksm, writes=["PS0"])
        p.op("DVE", TT(lgg, ps["PS0"][:, 0:4], gb, ALU.add), reads=["PS0"] + ksm, writes=ksm)
        p.op("DVE", RMAX(c0, lgg), reads=ksm, writes=["col0"])
        p.op("DVE", TS1(ohmp, lgg, c0, ALU.is_ge), reads=ksm + ["col0"], writes=ksm)
        p.op("PE", TRANS(ps["PS1"][0:4, 0:128], ohmp, sb["ident"]), reads=ksm + ["ident"], writes=["PS1"])
        p.op("DVE", CP(xf[0:4, 128 * ti: 128 * (ti + 1)], ps["PS1"][0:4, 0:128]),
             reads=["PS1"], writes=K("XF32", 128 * ti, 128 * (ti + 1)))
    p.dma("SP", [DMA(tids, hb["TIDS"][:])], writes=ksm)
    p.dma("POOL", [DMA(hb["idx_h"][:], hb["BIGIDX"][:])], writes=["idxh"])
    p.op("DVE", lambda h, o=cum, a=ohst: h.tensor_tensor_scan(o, a, a, 0.0, ALU.add, ALU.max),
         reads=koh, writes=kcum)
    p.op("DVE", TS(cum, cum, 1.0, float(PAD), ALU.subtract, ALU.min), reads=kcum, writes=kcum)
    p.op("DVE", TS1(cum, cum, sb["eoffc"][0:4, 0:1], ALU.add), reads=kcum + ["eoffc"], writes=kcum)
    p.op("DVE", TT(cum, cum, ohst, ALU.mult), reads=kcum + koh, writes=kcum)
    for s in range(T // 512):
        pm = "PS0" if s % 2 == 0 else "PS1"
        p.op("PE", MM(ps[pm][0:1, 0:512], sb["onesf"][0:4, 0:1],
                      cum[0:4, 512 * s: 512 * (s + 1)], True, True),
             reads=kcum + ["onesf"], writes=[pm])
        p.op("DVE", CP(comb[0:1, 512 * s: 512 * (s + 1)], ps[pm][0:1, 0:512]),
             reads=[pm], writes=K("XF32", 512 * s, 512 * (s + 1)))
    for tb in range((NT + 7) // 8):
        n_in = min(8, NT - 8 * tb)
        half = "ab"[tb % 2]
        grp = [TRANS(ps["PB0"][:, 512 * (tb % 2) + j: 512 * (tb % 2) + j + 1],
                     comb[0:1, 128 * (8 * tb + j): 128 * (8 * tb + j + 1)],
                     sb["ident"][0:1, 0:1]) for j in range(n_in)]
        p.op("PE", grp, reads=koh + ["ident"], writes=[f"PB0.{half}"])
        p.op("DVE", CP(posf[:, 8 * tb: 8 * tb + n_in],
                       ps["PB0"][:, 512 * (tb % 2): 512 * (tb % 2) + n_in]),
             reads=[f"PB0.{half}"], writes=ksm)
    p.op("DVE", CP(posi, posf), reads=ksm, writes=ksm)
    for tt_ in range(NT):
        p.dma("POOL", [lambda h, tt_=tt_: h.indirect_dma_start(
                           hb["idx_h"][:], IndirectOffsetOnAxis(posi[:, tt_:tt_ + 1], 0),
                           tids[:, tt_:tt_ + 1], None)],
              reads=ksm, writes=["idxh"])
    p.dma("POOL", [DMA(b2rb, hb["B2ROWS"][:])], writes=kb2)
    p.op("DVE", MEMSET(xgb[0], 0.0), writes=kxgb[0])
    p.op("DVE", MEMSET(xgb[1], 0.0), writes=kxgb[1])
    prefetch(0)
    emit_trans(0)
    for i, (e, ch) in enumerate(chunks):
        bi = i % 2
        xgfm = xgfmb[bi]
        kxgfm = kxgfmb[bi]
        hm01 = hmt[bi]
        khm01 = khmt[bi]
        b1c_ = b1cb[e % 2]
        kb1_ = kb1b[e % 2]
        if i + 1 < NCHK:
            prefetch(i + 1)
        nexte = e + 1 if (ch == NCHS - 1 and e < 3) else None

        def moe_mm1(ht):
            half = ht % 2
            pb = ps["PB0"][:, 512 * half: 512 * half + 256]
            w1keys = K("WHM", 1024 * ht, 1024 * (ht + 1))
            grp = [MM(pb, sb["WHM"][:, 1024 * ht + 128 * mc: 1024 * ht + 128 * (mc + 1)],
                      xgfm[:, 256 * mc: 256 * (mc + 1)], mc == 0, mc == 7) for mc in range(8)]
            p.op("PE", grp, reads=w1keys + kxgfm, writes=[f"PB0.{'ab'[half]}"])

        def moe_evac(ht):
            half = ht % 2
            pb = ps["PB0"][:, 512 * half: 512 * half + 256]
            hm_ = hm01[half]
            kh_ = [khm01[half]]
            if i < 2:
                kh_ = kh_ + (["SCR.4"] if bi == 0 else ["SCR.6"])
            bias_c = b1c_[:, ht:ht + 1]
            if half == 0:
                p.op("DVE", TS(hm_, pb, bias_c, 0.0, ALU.add, ALU.max),
                     reads=[f"PB0.{'ab'[half]}"] + kb1_, writes=kh_)
            else:
                p.op("ACT", ACTV(hm_, pb, AF.Relu, bias=bias_c),
                     reads=[f"PB0.{'ab'[half]}"] + kb1_, writes=kh_)

        def moe_mm2(ht):
            half = ht % 2
            hm_ = hm01[half]
            kh_ = [khm01[half]]
            for tt_ in range(2):
                for dh in range(2):
                    pm = f"PS{2 * tt_ + dh}"
                    p.op("PE", MM(ps[pm], hm_[:, 128 * tt_: 128 * (tt_ + 1)],
                                  sb["SL2"][:, 1024 * ht + 512 * dh: 1024 * ht + 512 * (dh + 1)],
                                  ht == 0, False),
                         reads=kh_ + [f"SL2.{ht}"], writes=[pm])

        moe_mm1(0)
        moe_evac(0)
        for ht in range(32):
            if ht + 1 < 32:
                moe_mm1(ht + 1)
                moe_evac(ht + 1)
            moe_mm2(ht)
            if nexte is not None and ht % 4 == 3:
                emit_weights(nexte, ht // 4)
                if ht == 3:
                    p.dma("SP", [DMA(b1cb[nexte % 2], hb["b1cols"][nexte])],
                          writes=kb1b[nexte % 2] + (["SCR.5"] if nexte == 1 else []))
        if i + 1 < NCHK:
            emit_trans(i + 1)
        for tt_ in range(2):
            for dh in range(2):
                pm = f"PS{2 * tt_ + dh}"
                p.op("PE", MM(ps[pm], sb["onesrbf"],
                              b2rb[0:1, 1024 * e + 512 * dh: 1024 * e + 512 * (dh + 1)],
                              False, True), reads=["onesrbf"] + kb2, writes=[pm])
                p.op("DVE", CP(ybuf[:, 1024 * tt_ + 512 * dh: 1024 * tt_ + 512 * (dh + 1)], ps[pm]),
                     reads=[pm], writes=kyb)
        for t2 in range(2):
            p.dma("POOL", [lambda h, t2=t2, bi=bi: h.indirect_dma_start(
                               hb["y_h"][:], IndirectOffsetOnAxis(idxb[bi][:, t2:t2 + 1], 0),
                               ybuf[:, 1024 * t2: 1024 * (t2 + 1)], None)],
                  reads=kyb + kidxb[bi], writes=["yall"])


def emit_final(cx):
    p, hb, sb, ps = cx.p, cx.hb, cx.sb, cx.ps
    T = cx.T
    NT = T // 128
    scr = sb["SCR"]

    def K(slab, lo, hi):
        return [f"{slab}.{b}" for b in range(lo // 1024, (hi + 1023) // 1024)]

    yt = scr[:, 0:1024]
    xa = scr[:, 1024:2048]
    x1 = scr[:, 2048:3072]
    s1 = scr[:, 3072:4096]
    ytb = scr.bitcast(BF16)[:, 8192:9216]
    ky, kxa = K("SCR", 0, 1024), K("SCR", 1024, 2048)
    kx1, ks1 = K("SCR", 2048, 3072), K("SCR", 3072, 4096)
    kyb_ = ["SCR.4"]
    for ti in range(NT):
        p.dma("SP", [DMA(ytb, hb["y_h"][128 * ti: 128 * (ti + 1), :]),
                     DMA(xa, hb["xa_h"][128 * ti: 128 * (ti + 1), :]),
                     DMA(x1, hb["x1_h"][128 * ti: 128 * (ti + 1), :]),
                     DMA(s1, hb["s1_h"][128 * ti: 128 * (ti + 1), :])],
              reads=["yall", f"xa.{ti}",
                     f"x1.{2 * ti}", f"x1.{2 * ti + 1}", f"s1.{ti}"],
              writes=kyb_ + kxa + kx1 + ks1)
        p.op("DVE", TT(yt, ytb, xa, ALU.add), reads=kyb_ + kxa, writes=ky)
        p.op("ACT", ACTV(yt, yt, AF.Silu), reads=ky, writes=ky)
        p.op("DVE", TT(yt, yt, s1, ALU.mult), reads=ky + ks1, writes=ky)
        p.op("DVE", TT(yt, yt, x1, ALU.add), reads=ky + kx1, writes=ky)
        p.dma("ACT", [DMA(hb["out"][128 * ti: 128 * (ti + 1), :], yt)],
              reads=ky, writes=[f"out.{ti}"])


def kernel(**inputs):
    T = 4096
    x = np.ascontiguousarray(np.asarray(inputs["L0_attn_x"], np.float32))
    host = prep_host(inputs, T)
    nc, cx = build_program(T)
    in_maps = []
    for i in range(8):
        m = dict(host)
        m["x"] = np.ascontiguousarray(x[i])
        in_maps.append(m)
    res = run_bass_kernel_spmd(nc, in_maps, core_ids=list(range(8)))
    return np.stack([np.asarray(r["out"], np.float32) for r in res.results], axis=0)
